# version 06
# baseline (speedup 1.0000x reference)
import numpy as np
import concourse.bass as bass
import concourse.mybir as mybir
from concourse.bass_utils import run_bass_kernel_spmd
from concourse.bass_core_rust import IndirectOffsetOnAxis

F32 = mybir.dt.float32
I32 = mybir.dt.int32
AF = mybir.ActivationFunctionType
ALU = mybir.AluOpType
AX = mybir.AxisListType

NCORE = 8
Bsz = 2
Tseq = 8192
Dm = 1024
Sst = 16
TOK = 2048
NT = Bsz * Tseq
KH = 1024
NHG = 2048
KEFF = 1152.0
NIT = 50
SLOT = 256
WIN = 512
XCP = 1028
EPS_LN = 1e-5
BIGRANK = 9999.0
SQD = float(np.sqrt(Dm))

A1 = float(np.float32(np.float32(np.log(np.float32(KEFF)))
                      - np.float32(np.log(np.float32(8192.0)))))
NLOGK = float(-np.float32(np.log(np.float32(KEFF))))


def build_nc():
    nc = bass.Bass("TRN2", target_bir_lowering=False, debug=False,
                   detect_race_conditions=False)

    x_tm = nc.dram_tensor("x_tm", [TOK, Dm], F32, kind="ExternalInput").ap()
    x_fm = nc.dram_tensor("x_fm", [Dm, TOK], F32, kind="ExternalInput").ap()
    wdT = nc.dram_tensor("wdT", [Dm, Dm], F32, kind="ExternalInput").ap()
    woT = nc.dram_tensor("woT", [Dm, Dm], F32, kind="ExternalInput").ap()
    w1lT = nc.dram_tensor("w1lT", [Dm, 512], F32, kind="ExternalInput").ap()
    w2lT = nc.dram_tensor("w2lT", [512, Dm], F32, kind="ExternalInput").ap()
    w1hb = nc.dram_tensor("w1hb", [32 * Dm, 128], F32, kind="ExternalInput").ap()
    w2hb = nc.dram_tensor("w2hb", [8 * 4096, 128], F32, kind="ExternalInput").ap()
    vecs = nc.dram_tensor("vecs", [128, 92], F32, kind="ExternalInput").ap()
    apb = nc.dram_tensor("apb", [128, 32], F32, kind="ExternalInput").ap()
    consts = nc.dram_tensor("consts", [128, 390], F32, kind="ExternalInput").ap()
    crow = nc.dram_tensor("crow", [1, 8], I32, kind="ExternalInput").ap()

    out_ext = nc.dram_tensor("out", [Dm, TOK], F32, kind="ExternalOutput").ap()

    ag_s_in = nc.dram_tensor("ag_s_in", [1, TOK], F32)
    ag_s_out = nc.dram_tensor("ag_s_out", [8, TOK], F32)
    agd_in = nc.dram_tensor("agd_in", [Dm, TOK], F32)
    agu_in = nc.dram_tensor("agu_in", [Dm, TOK], F32)
    agd_out = nc.dram_tensor("agd_out", [2 * 8 * 512, TOK], F32)
    agu_out = nc.dram_tensor("agu_out", [2 * 8 * 512, TOK], F32)
    agy_in = nc.dram_tensor("agy_in", [512, 4096], F32)
    agy_all = nc.dram_tensor("agy_all", [4 * 1024, 4096], F32)
    rank_dram = nc.dram_tensor("rank_dram", [Bsz, Tseq], F32)
    rk_loc = nc.dram_tensor("rk_loc", [1, TOK], F32)
    d_loc = nc.dram_tensor("d_loc", [8 * 128, TOK], F32)
    u_loc = nc.dram_tensor("u_loc", [8 * 128, TOK], F32)
    xsc = nc.dram_tensor("xsc", [NHG, XCP], F32)
    xsc_sum = nc.dram_tensor("xsc_sum", [NHG, XCP], F32)
    ag_h2_in = nc.dram_tensor("ag_h2_in", [SLOT, Dm], F32)
    h2full = nc.dram_tensor("h2full", [NHG + WIN, Dm], F32)

    sems = {}
    cnt = {}
    ctx = []

    def sem(name):
        cm = nc.semaphore(name)
        sems[name] = cm.__enter__()
        ctx.append(cm)
        cnt[name] = 0

    for n in ["sLdA", "sLdX", "sDW0", "sDW1", "sSw", "sScanL0", "sScanL1",
              "sYW0", "sYW1", "sYW2", "sYW3", "sMisc", "sZero", "sXrl",
              "sXscW", "sRkL", "sRflR", "sRfl0", "sRfl1",
              "sXgL", "sW1L", "sW2L", "sOwnL", "sFinW", "sFinY", "sFinX",
              "sOut", "sRhoL", "sH2W", "sPadW", "sRfl",
              "eStat", "eVar", "eXn", "eTrans", "eEvacXn", "eDelta", "eU",
              "eRho", "eLH1", "eLmul", "eSrow", "eSq", "rtd", "rta", "rtp",
              "btd", "bta", "btp", "eSel", "eRelB", "eExp", "eScan", "eSup",
              "eXgP", "eXgT", "eH1m", "eGelH", "eH2m", "eH2e", "eH2T",
              "eFinM", "eFin", "eYb", "eStt",
              "ccS", "ccDU", "ccY", "ccAR", "ccH2"]:
        sem(n)

    def inc(instr, name, amt):
        cnt[name] += amt
        instr.then_inc(sems[name], amt)
        return cnt[name]

    import contextlib
    _es = contextlib.ExitStack()
    big0 = _es.enter_context(nc.sbuf_tensor("big0", [128, 16384], F32))
    big1 = _es.enter_context(nc.sbuf_tensor("big1", [128, 8192], F32))
    big2 = _es.enter_context(nc.sbuf_tensor("big2", [128, 8192], F32))
    big4 = _es.enter_context(nc.sbuf_tensor("big4", [128, 4096], F32))
    big5 = _es.enter_context(nc.sbuf_tensor("big5", [128, 3072], F32))
    big6 = _es.enter_context(nc.sbuf_tensor("big6", [128, 3076], F32))
    ybig = _es.enter_context(nc.sbuf_tensor("ybig", [128, 4096], F32))
    xch = _es.enter_context(nc.sbuf_tensor("xch", [128, 1024], F32))
    rowb = _es.enter_context(nc.sbuf_tensor("rowb", [128, 2048], F32))
    Stt = _es.enter_context(nc.sbuf_tensor("Stt", [128, 2048], F32))
    mv = _es.enter_context(nc.sbuf_tensor("mv", [128, 32], F32))
    rhoc = _es.enter_context(nc.sbuf_tensor("rhoc", [128, 16], F32))
    vec_sb = _es.enter_context(nc.sbuf_tensor("vec_sb", [128, 92], F32))
    bstar = _es.enter_context(nc.sbuf_tensor("bstar", [128, 8], F32))
    apb_sb = _es.enter_context(nc.sbuf_tensor("apb_sb", [128, 32], F32))
    napos = _es.enter_context(nc.sbuf_tensor("napos", [128, 16], F32))
    cst = _es.enter_context(nc.sbuf_tensor("cst", [128, 390], F32))
    crow_sb = _es.enter_context(nc.sbuf_tensor("crow_sb", [1, 8], I32))
    s128 = _es.enter_context(nc.sbuf_tensor("s128", [128, 128], F32))
    w128 = _es.enter_context(nc.sbuf_tensor("w128", [128, 128], F32))
    m2 = _es.enter_context(nc.sbuf_tensor("m2", [128, 2], F32))
    na2 = _es.enter_context(nc.sbuf_tensor("na2", [128, 2], F32))
    mm2 = _es.enter_context(nc.sbuf_tensor("mm2", [128, 2], F32))
    t2a = _es.enter_context(nc.sbuf_tensor("t2a", [128, 2], F32))
    t2b = _es.enter_context(nc.sbuf_tensor("t2b", [128, 2], F32))
    rs2 = _es.enter_context(nc.sbuf_tensor("rs2", [128, 2], F32))
    tcol = _es.enter_context(nc.sbuf_tensor("tcol", [128, 4], F32))
    i2a = _es.enter_context(nc.sbuf_tensor("i2a", [128, 2], I32))
    i2b = _es.enter_context(nc.sbuf_tensor("i2b", [128, 2], I32))
    i2c = _es.enter_context(nc.sbuf_tensor("i2c", [128, 2], I32))
    i2d = _es.enter_context(nc.sbuf_tensor("i2d", [128, 2], I32))
    i2e = _es.enter_context(nc.sbuf_tensor("i2e", [128, 2], I32))
    thk = _es.enter_context(nc.sbuf_tensor("thk", [128, 2], I32))
    tt2 = _es.enter_context(nc.sbuf_tensor("tt2", [128, 2], F32))
    sc2 = _es.enter_context(nc.sbuf_tensor("sc2", [128, 2], F32))
    pD0 = _es.enter_context(nc.psum_tensor("pD0", [128, 512], F32))
    pD1 = _es.enter_context(nc.psum_tensor("pD1", [128, 512], F32))
    pL0 = _es.enter_context(nc.psum_tensor("pL0", [128, 512], F32))
    pL1 = _es.enter_context(nc.psum_tensor("pL1", [128, 512], F32))
    pF0 = _es.enter_context(nc.psum_tensor("pF0", [128, 512], F32))
    pF1 = _es.enter_context(nc.psum_tensor("pF1", [128, 512], F32))
    pTA = _es.enter_context(nc.psum_tensor("pTA", [128, 512], F32))
    pTB = _es.enter_context(nc.psum_tensor("pTB", [128, 512], F32))
    block = _es.enter_context(nc.Block())
    zrow = big6[:, 0:1028]
    XG_H = big6[:, 0:1028]
    XGN = big6[:, 1028:3076]
    scrC = big6[:, 1028:1540]
    scrH = big6[:, 1028:1540]
    xfin = big6[:, 0:1024]
    outt = big6[:, 1028:2052]
    dlu = ybig[:, 0:2048]
    h2tm = ybig[:, 2048:4096]
    rolrow = ybig[0:1, 2048:4096]
    scrB = Stt[:, 0:1024]
    sev = Stt[0:1, 1024:1536]
    rkrow = Stt[0:1, 0:2048]
    scrR = rowb[:, 0:80]
    t16 = w128[0:16, 0:128]
    e128 = Stt[:, 1024:1152]
    v128 = Stt[:, 1152:1280]
    tr2 = Stt[0:2, 1280:1408]
    ms1 = Stt[0:1, 1408:1410]
    pm1 = tcol[0:1, 1:2]
    pm1i = tcol[0:1, 2:3].bitcast(I32)
    rkc = ybig[:, 4:20]
    rkci = ybig[:, 20:36].bitcast(I32)
    state = ybig[:, 36:52]
    if True:
        pT = [pTA[:, 0:128], pTB[:, 0:128]]
        pR = pTA[:, 384:386]
        pS = pTB[0:1, 256:768] if False else pTB[0:1, 0:512]

        ONES = cst[:, 0:128]
        ONE_ROW = cst[0:1, 0:128]
        LSTR = cst[:, 128:256]
        IDN = cst[:, 256:384]
        IOTC = cst[:, 384:388]
        nc.const_aps.register(F32, EPS_LN, cst[:, 388:389])
        nc.const_aps.register(F32, -1e-4, cst[:, 389:390])
        DLT = [big0[:, 0:1024], big0[:, 1024:2048]]
        USB = [big0[:, 2048:3072], big0[:, 3072:4096]]
        ABR = [big0[:, 4096:5120], big0[:, 5120:6144]]
        GSB = [big0[:, 6144:7168], big0[:, 7168:8192]]
        YSB = [big0[:, 8192:9216], big0[:, 9216:10240]]
        H2OWN = big0[:, 10240:14336]
        s3 = s128[:].rearrange("p (b f) -> p b f", b=2)
        v3 = v128[:].rearrange("p (b f) -> p b f", b=2)
        e3 = e128[:].rearrange("p (b f) -> p b f", b=2)
        w3 = w128[:].rearrange("p (b f) -> p b f", b=2)
        key3 = w128[:].bitcast(I32).rearrange("p (b f) -> p b f", b=2)

        def bc2(ap2):
            return ap2.unsqueeze(2).to_broadcast([128, 2, 64])

        @block.sync
        def _(sync):
            inc(sync.dma_start(out=cst[:], in_=consts), "sLdA", 16)
            inc(sync.dma_start(out=vec_sb[:], in_=vecs), "sLdA", 16)
            inc(sync.dma_start(out=apb_sb[:], in_=apb), "sLdA", 16)
            inc(sync.dma_start(out=crow_sb[:], in_=crow), "sLdA", 16)
            for k in range(8):
                inc(sync.dma_start(out=big1[:, 1024 * k:1024 * (k + 1)],
                                   in_=wdT[128 * k:128 * (k + 1), :]), "sLdA", 16)
            for k in range(8):
                inc(sync.dma_start(out=big4[:, 512 * k:512 * (k + 1)],
                                   in_=w1lT[128 * k:128 * (k + 1), :]), "sLdA", 16)
            for ch in range(16):
                if ch >= 1:
                    sync.wait_ge(sems["eTrans"], 8 * ch)
                inc(sync.dma_start(out=xch[:],
                                   in_=x_tm[128 * ch:128 * (ch + 1), :]), "sLdX", 16)
        LDA_X = [16 * (ch + 1) for ch in range(16)]

        @block.scalar
        def _(scalar):
            scalar.wait_ge(sems["sLdA"], 16 * 20)
            scalar.activation(scrB[:, 0:16], apb_sb[:, 0:16], AF.Exp)
            scalar.drain()
            scalar.activation(scrB[:, 16:32], scrB[:, 0:16], AF.Ln, bias=1.0)
            scalar.drain()
            scalar.activation(napos[:], scrB[:, 16:32], AF.Copy,
                              bias=-1e-4, scale=-1.0)
            inc(scalar.drain(), "eRho", 1)

        @block.vector
        def _(vector):
            vector.wait_ge(sems["sLdA"], 16 * 20)
            vector.memset(zrow, 0.0)
            vector.memset(state[:], 0.0)
            inc(vector.tensor_add(bstar[:], vec_sb[:, 8:16], vec_sb[:, 20:28]),
                "eStat", 1)

        for ch in range(16):
            sl = xch[:]

            @block.vector
            def _(vector, ch=ch, sl=sl):
                vector.wait_ge(sems["sLdX"], LDA_X[ch])
                vector.reduce_sum(mv[:, 2 * ch:2 * ch + 1], sl, AX.X)
                inc(vector.drain(), "eStat", 1)

            @block.scalar
            def _(scalar, ch=ch, sl=sl):
                scalar.wait_ge(sems["sLdX"], LDA_X[ch])
                scalar.activation(big5[:, 0:1024], sl, AF.Square)
                inc(scalar.drain(), "eSq", 1)

            @block.vector
            def _(vector, ch=ch):
                vector.wait_ge(sems["eSq"], ch + 1)
                vector.reduce_sum(mv[:, 2 * ch + 1:2 * ch + 2],
                                  big5[:, 0:1024], AX.X)
                inc(vector.drain(), "eStat", 1)

            @block.vector
            def _(vector, ch=ch):
                vector.wait_ge(sems["eStat"], 1 + 2 * (ch + 1))
                m_ = mv[:, 2 * ch:2 * ch + 1]
                v_ = mv[:, 2 * ch + 1:2 * ch + 2]
                vector.tensor_scalar_mul(m_, m_, 1.0 / Dm)
                vector.tensor_scalar_mul(v_, v_, 1.0 / Dm)
                vector.drain()
                vector.tensor_mul(tcol[:, 0:1], m_, m_)
                vector.drain()
                vector.tensor_sub(v_, v_, tcol[:, 0:1])
                inc(vector.drain(), "eVar", 1)

            @block.scalar
            def _(scalar, ch=ch):
                scalar.wait_ge(sems["eVar"], ch + 1)
                scalar.activation(tcol[:, 1:2], mv[:, 2 * ch + 1:2 * ch + 2],
                                  AF.Ln, bias=EPS_LN)
                scalar.drain()
                scalar.activation(rhoc[:, ch:ch + 1], tcol[:, 1:2], AF.Exp,
                                  scale=-0.5)
                inc(scalar.drain(), "eRho", 1)

            @block.vector
            def _(vector, ch=ch, sl=sl):
                vector.wait_ge(sems["eRho"], 1 + ch + 1)
                vector.tensor_scalar(sl, sl, mv[:, 2 * ch:2 * ch + 1],
                                     rhoc[:, ch:ch + 1], ALU.subtract, ALU.mult)
                inc(vector.drain(), "eXn", 1)

            @block.tensor
            def _(tensor, ch=ch, sl=sl):
                tensor.wait_ge(sems["eXn"], ch + 1)
                for j in range(8):
                    idx = ch * 8 + j
                    if idx >= 2:
                        tensor.wait_ge(sems["eEvacXn"], idx - 1)
                    inc(tensor.transpose(pT[idx % 2], sl[:, 128 * j:128 * (j + 1)],
                                         IDN), "eTrans", 1)

            @block.scalar
            def _(scalar, ch=ch):
                for j in range(8):
                    idx = ch * 8 + j
                    scalar.wait_ge(sems["eTrans"], idx + 1)
                    scalar.copy(
                        big0[:, 2048 * j + 128 * ch:2048 * j + 128 * (ch + 1)],
                        pT[idx % 2])
                    inc(scalar.drain(), "eEvacXn", 1)

        @block.scalar
        def _(scalar):
            scalar.wait_ge(sems["eVar"], 16)
            vcols = mv[:].rearrange("p (c two) -> p two c", two=2)[:, 1, :]
            scalar.activation(scrR[:, 0:16], vcols, AF.Ln, bias=EPS_LN)
            scalar.drain()
            scalar.activation(scrR[:, 16:32], scrR[:, 0:16], AF.Exp, scale=-1.0)
            inc(scalar.drain(), "eRho", 1)

        @block.vector
        def _(vector):
            vector.wait_ge(sems["eRho"], 18)
            vcols = mv[:].rearrange("p (c two) -> p two c", two=2)[:, 1, :]
            vector.tensor_mul(scrR[:, 32:48], vcols, scrR[:, 16:32])
            vector.drain()
            vector.tensor_scalar_mul(scrR[:, 32:48], scrR[:, 32:48], float(Dm))
            vector.drain()
            vector.tensor_scalar_max(scrR[:, 32:48], scrR[:, 32:48], 1e-24)
            inc(vector.drain(), "eLmul", 1)

        @block.scalar
        def _(scalar):
            scalar.wait_ge(sems["eLmul"], 1)
            scalar.activation(scrR[:, 48:64], scrR[:, 32:48], AF.Ln)
            scalar.drain()
            scalar.activation(scrR[:, 64:80], scrR[:, 48:64], AF.Exp, scale=-0.5)
            inc(scalar.drain(), "eRho", 1)

        @block.tensor
        def _(tensor):
            tensor.wait_ge(sems["eRho"], 19)
            tensor.wait_ge(sems["eEvacXn"], 127)
            inc(tensor.transpose(pTA[0:16, 0:128], scrR[:, 64:80], IDN), "eTrans", 1)
        TR_RHO = cnt["eTrans"]

        @block.scalar
        def _(scalar):
            scalar.wait_ge(sems["eTrans"], TR_RHO)
            scalar.copy(t16, pTA[0:16, 0:128])
            inc(scalar.drain(), "eEvacXn", 1)
        EV_RHO = cnt["eEvacXn"]

        @block.scalar
        def _(scalar):
            scalar.wait_ge(sems["eEvacXn"], EV_RHO)
            for ch in range(16):
                amt = inc(scalar.dma_start(out=rolrow[0:1, 128 * ch:128 * (ch + 1)],
                                           in_=t16[ch:ch + 1, 0:128]), "sRhoL", 16)

        @block.tensor
        def _(tensor):
            tensor.wait_ge(sems["sRhoL"], 256)
            for q in range(4):
                pf = pF0 if q % 2 == 0 else pF1
                if q >= 2:
                    tensor.wait_ge(sems["eLmul"], q)
                inc(tensor.matmul(pf[:, 0:512], ONE_ROW,
                                  rolrow[0:1, 512 * q:512 * (q + 1)],
                                  start=True, stop=True), "eYb", 1)

        @block.vector
        def _(vector):
            for q in range(4):
                pf = pF0 if q % 2 == 0 else pF1
                vector.wait_ge(sems["eYb"], q + 1)
                vector.tensor_copy(rowb[:, 512 * q:512 * (q + 1)], pf[:, 0:512])
                inc(vector.drain(), "eLmul", 1)

        @block.vector
        def _(vector):
            vector.wait_ge(sems["sLdA"], 16 * 20)
            for k in range(8):
                vector.tensor_scalar_mul(tcol[:, 2:3], vec_sb[:, 84 + k:85 + k], SQD)
                vector.drain()
                vector.tensor_scalar_mul(big4[:, 512 * k:512 * (k + 1)],
                                         big4[:, 512 * k:512 * (k + 1)],
                                         tcol[:, 2:3])
                inc(vector.drain(), "eLmul", 1)

        TR_S0 = cnt["eTrans"]
        for tt in range(4):

            @block.tensor
            def _(tensor, tt=tt):
                if tt == 0:
                    tensor.wait_ge(sems["eEvacXn"], 128)
                if tt >= 1:
                    tensor.wait_ge(sems["eSrow"], tt)
                for k in range(8):
                    inc(tensor.matmul(pS, vec_sb[:, 76 + k:77 + k],
                                      big0[:, 2048 * k + 512 * tt:2048 * k + 512 * (tt + 1)],
                                      start=(k == 0), stop=(k == 7)), "eTrans", 1)

            @block.vector
            def _(vector, tt=tt):
                vector.wait_ge(sems["eTrans"], TR_S0 + 8 * (tt + 1))
                if tt >= 1:
                    vector.wait_ge(sems["sSw"], 16 * tt)
                vector.tensor_copy(sev, pS)
                inc(vector.drain(), "eSrow", 1)

            @block.sync
            def _(sync, tt=tt):
                sync.wait_ge(sems["eSrow"], tt + 1)
                inc(sync.dma_start(out=ag_s_in.ap()[0:1, 512 * tt:512 * (tt + 1)],
                                   in_=sev), "sSw", 16)

        @block.sync
        def _(sync):
            sync.wait_ge(sems["ccS"], 1)
            for b in range(Bsz):
                inc(sync.dma_start(
                    out=s128[:, 64 * b:64 * (b + 1)],
                    in_=ag_s_out.ap()[4 * b:4 * b + 4, :]
                        .rearrange("r t -> (r t)").rearrange("(p f) -> p f", p=128)),
                    "sSw", 16)

        TR_D0 = cnt["eTrans"]

        def emit_delta(g):
            if True:
                ech, tt = g // 4, g % 4
                pd = pD0 if g % 2 == 0 else pD1

                @block.tensor
                def _(tensor, tt=tt, ech=ech, g=g, pd=pd):
                    tensor.wait_ge(sems["eEvacXn"], 8 * (4 * tt + 4))
                    if g >= 2:
                        tensor.wait_ge(sems["eDelta"], g - 1)
                    for k in range(8):
                        inc(tensor.matmul(
                            pd[:, 0:512],
                            big1[:, 1024 * k + 128 * ech:1024 * k + 128 * (ech + 1)],
                            big0[:, 2048 * k + 512 * tt:2048 * k + 512 * (tt + 1)],
                            start=(k == 0), stop=(k == 7)), "eTrans", 1)

                @block.scalar
                def _(scalar, g=g, ech=ech, pd=pd):
                    scalar.wait_ge(sems["eTrans"], TR_D0 + 8 * (g + 1))
                    if g >= 2:
                        scalar.wait_ge(sems["eU"], g - 1)
                        scalar.wait_ge(sems["sDW0" if g % 2 == 0 else "sDW1"],
                                       32 * ((g - 2) // 2 + 1))
                    dl = dlu[:, 1024 * (g % 2):1024 * (g % 2) + 512]
                    scalar.activation(scrB[:, 0:512], pd[:, 0:512], AF.Exp,
                                      bias=vec_sb[:, ech:ech + 1])
                    scalar.drain()
                    scalar.activation(dl, scrB[:, 0:512], AF.Ln, bias=1.0)
                    inc(scalar.drain(), "eDelta", 1)

                @block.vector
                def _(vector, tt=tt, ech=ech, g=g):
                    dl = dlu[:, 1024 * (g % 2):1024 * (g % 2) + 512]
                    ut = dlu[:, 1024 * (g % 2) + 512:1024 * (g % 2) + 1024]
                    vector.wait_ge(sems["eDelta"], g + 1)
                    vector.tensor_mul(
                        ut, dl,
                        big0[:, 2048 * ech + 512 * tt:2048 * ech + 512 * (tt + 1)])
                    inc(vector.drain(), "eU", 1)

        @block.sync
        def _(sync):
            for ech in range(8):
                for tt in range(4):
                    g = ech * 4 + tt
                    dl = dlu[:, 1024 * (g % 2):1024 * (g % 2) + 512]
                    ut = dlu[:, 1024 * (g % 2) + 512:1024 * (g % 2) + 1024]
                    sync.wait_ge(sems["eU"], g + 1)
                    par = "sDW0" if g % 2 == 0 else "sDW1"
                    inc(sync.dma_start(
                        out=agd_in.ap()[128 * ech:128 * (ech + 1),
                                        512 * tt:512 * (tt + 1)], in_=dl), par, 16)
                    inc(sync.dma_start(
                        out=agu_in.ap()[128 * ech:128 * (ech + 1),
                                        512 * tt:512 * (tt + 1)], in_=ut), par, 16)

        @block.gpsimd
        def _(gpsimd):
            gpsimd.wait_ge(sems["sSw"], 64)
            inc(gpsimd.collective_compute(
                "AllGather", ALU.bypass, replica_groups=[list(range(NCORE))],
                ins=[ag_s_in.ap().opt()], outs=[ag_s_out.ap().opt()]), "ccS", 1)
            for h in range(2):
                gpsimd.wait_ge(sems["sDW0"], 256 if h == 0 else 512)
                gpsimd.wait_ge(sems["sDW1"], 256 if h == 0 else 512)
                inc(gpsimd.collective_compute(
                    "AllGather", ALU.bypass, replica_groups=[list(range(NCORE))],
                    ins=[agd_in.ap()[512 * h:512 * (h + 1), :].opt()],
                    outs=[agd_out.ap()[4096 * h:4096 * (h + 1), :].opt()]), "ccDU", 1)
                inc(gpsimd.collective_compute(
                    "AllGather", ALU.bypass, replica_groups=[list(range(NCORE))],
                    ins=[agu_in.ap()[512 * h:512 * (h + 1), :].opt()],
                    outs=[agu_out.ap()[4096 * h:4096 * (h + 1), :].opt()]), "ccDU", 1)

        @block.vector
        def _(vector):
            vector.wait_ge(sems["sSw"], 96)
            for b in range(Bsz):
                vector.reduce_max(t2a[:, b:b + 1], s128[:, 64 * b:64 * (b + 1)], AX.X)
            vector.drain()
            vector.tensor_copy(t2b[:, 0:2], t2a[:, 0:2])
            inc(vector.drain(), "rtd", 1)

        @block.tensor
        def _(tensor):
            tensor.wait_ge(sems["rtd"], 1)
            tensor.wait_ge(sems["eEvacXn"], EV_RHO)
            inc(tensor.transpose(pTA[0:2, 0:128], t2b[:, 0:2], IDN), "rtp", 1)

        @block.vector
        def _(vector):
            vector.wait_ge(sems["rtp"], 1)
            vector.tensor_copy(tr2[:, 0:128], pTA[0:2, 0:128])
            vector.drain()
            vector.reduce_max(t2a[0:2, 0:1], tr2[:, 0:128], AX.X)
            inc(vector.drain(), "rtd", 1)

        @block.tensor
        def _(tensor):
            tensor.wait_ge(sems["rtd"], 2)
            inc(tensor.transpose(pTB[0:1, 0:2], t2a[0:2, 0:1], IDN[0:2, 0:2]), "rtp", 1)

        @block.vector
        def _(vector):
            vector.wait_ge(sems["rtp"], 2)
            vector.tensor_copy(ms1[0:1, 0:2], pTB[0:1, 0:2])
            inc(vector.drain(), "rtd", 1)

        @block.tensor
        def _(tensor):
            tensor.wait_ge(sems["rtd"], 3)
            inc(tensor.matmul(pR, ONE_ROW, ms1[0:1, 0:2], start=True, stop=True),
                "rtp", 1)

        @block.vector
        def _(vector):
            vector.wait_ge(sems["rtp"], 3)
            vector.tensor_copy(m2[:, 0:2], pR)
            vector.memset(na2[:], -A1)
            vector.drain()
            vector.tensor_tensor(mm2[:], m2[:], na2[:], ALU.min)
            inc(vector.drain(), "rtd", 1)

        RTD0, RTA0, RTP0 = cnt["rtd"], cnt["rta"], cnt["rtp"]
        for j in range(NIT - 1):

            @block.vector
            def _(vector, j=j):
                if j == 0:
                    vector.wait_ge(sems["sRhoL"], 256)
                if j > 0:
                    vector.wait_ge(sems["rta"], RTA0 + 2 * j)
                    vector.scalar_tensor_tensor(na2[:], t2a[:, 0:2], NLOGK,
                                                mm2[:], ALU.add, ALU.add)
                    vector.drain()
                    vector.tensor_tensor(mm2[:], m2[:], na2[:], ALU.min)
                    vector.drain()
                vector.tensor_tensor(v3, s3, bc2(na2[:]), ALU.min)
                vector.drain()
                vector.tensor_tensor(w3, v3, bc2(mm2[:]), ALU.subtract)
                inc(vector.drain(), "rtd", 1)

            @block.scalar
            def _(scalar, j=j):
                scalar.wait_ge(sems["rtd"], RTD0 + 2 * j + 1)
                scalar.activation(e128[:], w128[:], AF.Exp)
                inc(scalar.drain(), "rta", 1)

            @block.vector
            def _(vector, j=j):
                vector.wait_ge(sems["rta"], RTA0 + 2 * j + 1)
                vector.reduce_sum(rs2[:, 0:2], e3, AX.X)
                inc(vector.drain(), "rtd", 1)

            @block.tensor
            def _(tensor, j=j):
                tensor.wait_ge(sems["rtd"], RTD0 + 2 * j + 2)
                if j > 0:
                    tensor.wait_ge(sems["rta"], RTA0 + 2 * j)
                inc(tensor.matmul(pR, ONES, rs2[:, 0:2], start=True, stop=True),
                    "rtp", 1)

            @block.scalar
            def _(scalar, j=j):
                scalar.wait_ge(sems["rtp"], RTP0 + j + 1)
                scalar.activation(t2a[:, 0:2], pR, AF.Ln)
                inc(scalar.drain(), "rta", 1)

            if j < 32:
                emit_delta(j)

        @block.vector
        def _(vector):
            vector.wait_ge(sems["rta"], RTA0 + 2 * (NIT - 1))
            vector.scalar_tensor_tensor(na2[:], t2a[:, 0:2], NLOGK, mm2[:],
                                        ALU.add, ALU.add)
            vector.drain()
            vector.tensor_tensor(v3, s3, bc2(na2[:]), ALU.subtract)
            vector.drain()
            vector.tensor_scalar_min(v128[:], v128[:], 0.0)
            vector.memset(w128[:], 0.0)
            vector.drain()
            vector.tensor_sub(w128[:], w128[:], v128[:])
            vector.memset(i2a[:], 0)
            vector.memset(i2b[:], 2139095039)
            inc(vector.drain(), "btd", 1)

        def bisect_hi_update(vector):
            vector.tensor_single_scalar(sc2[:], t2b[:, 0:2], float(KH), ALU.is_ge)
            vector.drain()
            vector.tensor_copy(i2e[:], sc2[:])
            vector.tensor_sub(i2d[:], i2c[:], i2b[:])
            vector.drain()
            vector.tensor_mul(i2d[:], i2d[:], i2e[:])
            vector.drain()
            vector.tensor_add(i2b[:], i2b[:], i2d[:])
            vector.drain()

        for j in range(31):

            @block.vector
            def _(vector, j=j):
                if j > 0:
                    vector.wait_ge(sems["bta"], j)
                    bisect_hi_update(vector)
                    vector.tensor_scalar(i2e[:], i2e[:], -1, 1, ALU.mult, ALU.add)
                    vector.tensor_sub(i2d[:], i2c[:], i2a[:])
                    vector.drain()
                    vector.tensor_scalar_add(i2d[:], i2d[:], 1)
                    vector.drain()
                    vector.tensor_mul(i2d[:], i2d[:], i2e[:])
                    vector.drain()
                    vector.tensor_add(i2a[:], i2a[:], i2d[:])
                    vector.drain()
                vector.tensor_sub(i2c[:], i2b[:], i2a[:])
                vector.drain()
                vector.tensor_single_scalar(i2c[:], i2c[:], 1, ALU.logical_shift_right)
                vector.drain()
                vector.tensor_add(i2c[:], i2c[:], i2a[:])
                vector.drain()
                vector.tensor_tensor(e3, key3, bc2(i2c[:]), ALU.is_le)
                vector.drain()
                vector.reduce_sum(rs2[:, 0:2], e3, AX.X)
                inc(vector.drain(), "btd", 1)

            @block.tensor
            def _(tensor, j=j):
                tensor.wait_ge(sems["btd"], j + 2)
                if j > 0:
                    tensor.wait_ge(sems["bta"], j)
                inc(tensor.matmul(pR, ONES, rs2[:, 0:2], start=True, stop=True),
                    "btp", 1)

            @block.scalar
            def _(scalar, j=j):
                scalar.wait_ge(sems["btp"], j + 1)
                scalar.copy(t2b[:, 0:2], pR)
                inc(scalar.drain(), "bta", 1)

        @block.vector
        def _(vector):
            vector.wait_ge(sems["bta"], 31)
            bisect_hi_update(vector)
            vector.tensor_copy(thk[:], i2b[:])
            vector.drain()
            vector.tensor_tensor(e3, key3, bc2(thk[:]), ALU.is_lt)
            vector.drain()
            vector.reduce_sum(rs2[:, 0:2], e3, AX.X)
            inc(vector.drain(), "btd", 1)

        @block.tensor
        def _(tensor):
            tensor.wait_ge(sems["btd"], 33)
            tensor.wait_ge(sems["bta"], 31)
            inc(tensor.matmul(pR, ONES, rs2[:, 0:2], start=True, stop=True), "btp", 1)

        @block.scalar
        def _(scalar):
            scalar.wait_ge(sems["btp"], 32)
            scalar.copy(t2b[:, 0:2], pR)
            inc(scalar.drain(), "bta", 1)

        @block.vector
        def _(vector):
            vector.wait_ge(sems["bta"], 32)
            vector.tensor_scalar(tt2[:], t2b[:, 0:2], -1.0, float(KH),
                                 ALU.mult, ALU.add)
            vector.tensor_tensor(e3, key3, bc2(thk[:]), ALU.is_lt)
            vector.tensor_tensor(v3, key3, bc2(thk[:]), ALU.is_equal)
            vector.drain()
            for b in range(Bsz):
                vector.tensor_tensor_scan(w128[:, 64 * b:64 * (b + 1)],
                                          ONES[:, 0:64], v128[:, 64 * b:64 * (b + 1)],
                                          0.0, ALU.mult, ALU.add)
            vector.drain()
            vector.tensor_copy(rs2[:, 0:1], w128[:, 63:64])
            vector.tensor_copy(rs2[:, 1:2], w128[:, 127:128])
            inc(vector.drain(), "eSel", 1)

        @block.tensor
        def _(tensor):
            tensor.wait_ge(sems["eSel"], 1)
            inc(tensor.matmul(pR, LSTR, rs2[:, 0:2], start=True, stop=True),
                "eSel", 1)

        @block.vector
        def _(vector):
            vector.wait_ge(sems["eSel"], 2)
            vector.tensor_copy(t2a[:, 0:2], pR)
            vector.drain()
            vector.tensor_sub(w128[:], w128[:], v128[:])
            vector.drain()
            vector.tensor_tensor(w3, w3, bc2(t2a[:, 0:2]), ALU.add)
            vector.drain()
            vector.tensor_tensor(w3, w3, bc2(tt2[:]), ALU.is_lt)
            vector.drain()
            vector.tensor_mul(v128[:], v128[:], w128[:])
            vector.drain()
            vector.tensor_add(e128[:], e128[:], v128[:])
            vector.drain()
            for b in range(Bsz):
                vector.tensor_tensor_scan(w128[:, 64 * b:64 * (b + 1)],
                                          ONES[:, 0:64], e128[:, 64 * b:64 * (b + 1)],
                                          0.0, ALU.mult, ALU.add)
            vector.drain()
            vector.tensor_copy(rs2[:, 0:1], w128[:, 63:64])
            vector.tensor_copy(rs2[:, 1:2], w128[:, 127:128])
            inc(vector.drain(), "eSel", 1)

        @block.tensor
        def _(tensor):
            tensor.wait_ge(sems["eSel"], 3)
            inc(tensor.matmul(pR, LSTR, rs2[:, 0:2], start=True, stop=True),
                "eSel", 1)

        @block.vector
        def _(vector):
            vector.wait_ge(sems["eSel"], 4)
            vector.tensor_copy(t2a[:, 0:2], pR)
            vector.drain()
            vector.tensor_sub(w128[:], w128[:], e128[:])
            vector.drain()
            vector.tensor_tensor(w3, w3, bc2(t2a[:, 0:2]), ALU.add)
            vector.drain()
            vector.tensor_scalar_add(w128[:, 64:128], w128[:, 64:128], float(KH))
            vector.drain()
            vector.tensor_scalar_add(w128[:], w128[:], -BIGRANK)
            vector.drain()
            vector.tensor_mul(w128[:], w128[:], e128[:])
            vector.drain()
            vector.tensor_scalar_add(w128[:], w128[:], BIGRANK)
            inc(vector.drain(), "eSel", 1)

        @block.sync
        def _(sync):
            sync.wait_ge(sems["eSel"], 5)
            for b in range(Bsz):
                inc(sync.dma_start(
                    out=rank_dram.ap()[b, :].rearrange("(p f) -> p f", p=128),
                    in_=w128[:, 64 * b:64 * (b + 1)]), "sMisc", 16)
        MISC_RANK = cnt["sMisc"]

        TR_L0 = cnt["eTrans"]
        for hch in range(4):
            for tt in range(4):
                g = hch * 4 + tt
                pl = pL0 if g % 2 == 0 else pL1

                @block.tensor
                def _(tensor, hch=hch, tt=tt, g=g, pl=pl):
                    tensor.wait_ge(sems["eLmul"], 13)
                    if g >= 2:
                        tensor.wait_ge(sems["eLmul"], 13 + g - 1)
                    for k in range(8):
                        inc(tensor.matmul(
                            pl[:, 0:512],
                            big4[:, 512 * k + 128 * hch:512 * k + 128 * (hch + 1)],
                            big0[:, 2048 * k + 512 * tt:2048 * k + 512 * (tt + 1)],
                            start=(k == 0), stop=(k == 7)), "eTrans", 1)

                @block.vector
                def _(vector, g=g, tt=tt, pl=pl):
                    vector.wait_ge(sems["eTrans"], TR_L0 + 8 * (g + 1))
                    if g >= 2:
                        vector.wait_ge(sems["eLH1"], g - 1)
                    vector.tensor_mul(scrC, pl[:, 0:512],
                                      rowb[:, 512 * tt:512 * (tt + 1)])
                    inc(vector.drain(), "eLmul", 1)

                @block.scalar
                def _(scalar, hch=hch, tt=tt, g=g):
                    scalar.wait_ge(sems["eLmul"], 13 + g + 1)
                    scalar.activation(
                        big2[:, 2048 * hch + 512 * tt:2048 * hch + 512 * (tt + 1)],
                        scrC, AF.Gelu,
                        bias=vec_sb[:, 16 + hch:17 + hch])
                    inc(scalar.drain(), "eLH1", 1)

        @block.scalar
        def _(scalar):
            scalar.wait_ge(sems["sMisc"], MISC_RANK)
            g0 = scalar.value_load(crow_sb[0:1, 2:3])
            rd_flat = rank_dram.ap().rearrange("b t -> (b t)")
            inc(scalar.dma_start(out=rk_loc.ap(),
                                 in_=bass.AP(tensor=rd_flat.tensor, offset=g0,
                                             ap=[[2048, 1], [1, 2048]])), "sRflR", 16)
            scalar.wait_ge(sems["sRflR"], 16)
            inc(scalar.dma_start(out=rkrow[0:1, :], in_=rk_loc.ap()), "sRkL", 16)
            for i in range(16):
                inc(scalar.dma_start(
                    out=rkc[:, i:i + 1],
                    in_=rk_loc.ap()[0, 128 * i:128 * (i + 1)]
                        .rearrange("(p o) -> p o", o=1)), "sRkL", 16)

        @block.vector
        def _(vector):
            vector.wait_ge(sems["sRkL"], 16 * 17)
            vector.tensor_reduce(pm1[0:1, 0:1], rkrow[0:1, :], AX.X, ALU.min)
            vector.drain()
            vector.tensor_scalar_min(pm1[0:1, 0:1], pm1[0:1, 0:1], float(NHG - 1))
            vector.drain()
            vector.tensor_copy(pm1i[0:1, 0:1], pm1[0:1, 0:1])
            vector.tensor_single_scalar(rkrow[0:1, :], rkrow[0:1, :],
                                        pm1[0:1, 0:1], ALU.subtract)
            vector.tensor_copy(rkci[:], rkc[:])
            inc(vector.drain(), "eSel", 1)

        @block.sync
        def _(sync):
            for i in range(16):
                inc(sync.dma_start(out=xsc.ap()[128 * i:128 * (i + 1), :],
                                   in_=zrow), "sZero", 16)
            for i in range(4):
                inc(sync.dma_start(out=h2full.ap()[NHG + 128 * i:NHG + 128 * (i + 1), :],
                                   in_=zrow[:, 0:Dm]), "sPadW", 16)
            for i in range(16):
                if i == 0:
                    sync.wait_ge(sems["eTrans"], 128)
                if i >= 1:
                    sync.wait_ge(sems["sXscW"], 32 * i)
                inc(sync.dma_start(out=xch[:],
                                   in_=x_tm[128 * i:128 * (i + 1), :]), "sXrl", 16)

        @block.gpsimd
        def _(gpsimd):
            gpsimd.wait_ge(sems["eSel"], 6)
            gpsimd.wait_ge(sems["sZero"], 256)
            for i in range(16):
                gpsimd.wait_ge(sems["sXrl"], 16 * (i + 1))
                inc(gpsimd.indirect_dma_start(
                    xsc.ap(), IndirectOffsetOnAxis(rkci[:, i:i + 1], 0),
                    xch[:], None,
                    bounds_check=NHG - 1, oob_is_err=False), "sXscW", 16)
                inc(gpsimd.indirect_dma_start(
                    xsc.ap(), IndirectOffsetOnAxis(rkci[:, i:i + 1], 0),
                    mv[:, 2 * i:2 * i + 2], None, element_offset=1024,
                    bounds_check=NHG - 1, oob_is_err=False), "sXscW", 16)
            gpsimd.wait_ge(sems["sXscW"], 16 * 32)
            for q in range(4):
                inc(gpsimd.collective_compute(
                    "AllReduce", ALU.add, replica_groups=[list(range(NCORE))],
                    ins=[xsc.ap()[512 * q:512 * (q + 1), :].opt()],
                    outs=[xsc_sum.ap()[512 * q:512 * (q + 1), :].opt()]), "ccAR", 1)

        @block.scalar
        def _(scalar):
            e0k = scalar.value_load(crow_sb[0:1, 1:2])
            scalar.wait_ge(sems["ccDU"], 4)
            inc(scalar.dma_start(
                out=d_loc.ap().rearrange("(r p) f -> r p f", r=8),
                in_=bass.AP(tensor=agd_out.ap().tensor, offset=e0k,
                            ap=[[1024 * 1024, 8], [2048, 128], [1, 2048]])),
                "sRfl0", 16)
            inc(scalar.dma_start(
                out=u_loc.ap().rearrange("(r p) f -> r p f", r=8),
                in_=bass.AP(tensor=agu_out.ap().tensor, offset=e0k,
                            ap=[[1024 * 1024, 8], [2048, 128], [1, 2048]])),
                "sRfl0", 16)

            def emit_load(sup):
                r, h = sup // 2, sup % 2
                if sup >= 2:
                    scalar.wait_ge(sems["eSup"], sup - 1)
                scalar.wait_ge(sems["sRfl0"], 32)
                par = sup % 2
                sl_ = "sScanL0" if par == 0 else "sScanL1"
                inc(scalar.dma_start(
                    out=DLT[par],
                    in_=d_loc.ap()[128 * r:128 * (r + 1),
                                   1024 * h:1024 * (h + 1)]), sl_, 16)
                inc(scalar.dma_start(
                    out=USB[par],
                    in_=u_loc.ap()[128 * r:128 * (r + 1),
                                   1024 * h:1024 * (h + 1)]), sl_, 16)

            emit_load(0)
            emit_load(1)
            for sup in range(16):
                for s in range(Sst):
                    idx = sup * 16 + s
                    scalar.wait_ge(sems["sScanL0" if sup % 2 == 0 else "sScanL1"],
                                   32 * (sup // 2 + 1))
                    if idx >= 2:
                        scalar.wait_ge(sems["eScan"], idx - 1)
                    inc(scalar.activation(ABR[idx % 2], DLT[sup % 2], AF.Exp,
                                          scale=napos[:, s:s + 1]), "eExp", 1)
                    if s == 0 and sup >= 1:
                        scalar.wait_ge(sems["eSup"], sup)
                        su = sup - 1
                        inc(scalar.dma_start(
                            out=agy_in.ap()[128 * (su // 4):128 * (su // 4) + 128,
                                            1024 * (su % 4):1024 * (su % 4) + 1024],
                            in_=YSB[1]), "sYW%d" % (su // 4), 16)
                if sup + 2 < 16:
                    emit_load(sup + 2)
            scalar.wait_ge(sems["eSup"], 16)
            inc(scalar.dma_start(out=agy_in.ap()[384:512, 3072:4096],
                                 in_=YSB[1]), "sYW3", 16)

        for sup in range(16):

            @block.vector
            def _(vector, sup=sup):
                par = sup % 2
                for s in range(Sst):
                    idx = sup * 16 + s
                    vector.wait_ge(sems["eExp"], idx + 1)
                    if s == 0 and sup % 8 == 0:
                        vector.memset(state[:], 0.0)
                        vector.drain()
                    if s == 1 and sup >= 1:
                        vector.wait_ge(sems["sYW%d" % ((sup - 1) // 4)],
                                       16 * ((sup - 1) % 4 + 1))
                    inc(vector.tensor_tensor_scan(
                        GSB[idx % 2], ABR[idx % 2], USB[par],
                        state[:, s:s + 1], ALU.mult, ALU.add), "eScan", 1)
                    if s == 0:
                        vector.tensor_scalar_mul(YSB[0], GSB[idx % 2],
                                                 apb_sb[:, 16 + s:17 + s])
                    else:
                        vector.scalar_tensor_tensor(
                            YSB[s % 2], GSB[idx % 2], apb_sb[:, 16 + s:17 + s],
                            YSB[(s + 1) % 2], ALU.mult, ALU.add)
                    vector.tensor_copy(state[:, s:s + 1], GSB[idx % 2][:, 1023:1024])
                vector.tensor_copy(tcol[:, 3:4], state[:, 0:1])
                inc(vector.drain(), "eSup", 1)

        @block.gpsimd
        def _(gpsimd):
            for q in range(4):
                gpsimd.wait_ge(sems["sYW%d" % q], 16 * 4)
                inc(gpsimd.collective_compute(
                    "AllGather", ALU.bypass, replica_groups=[list(range(NCORE))],
                    ins=[agy_in.ap()[128 * q:128 * (q + 1), :].opt()],
                    outs=[agy_all.ap()[1024 * q:1024 * (q + 1), :].opt()]), "ccY", 1)

        for k in range(2):

            @block.scalar
            def _(scalar, k=k):
                if k == 0:
                    scalar.wait_ge(sems["ccAR"], 4)
                else:
                    scalar.wait_ge(sems["eXgT"], 8 * k)
                s0x = scalar.value_load(crow_sb[0:1, 3:4])
                inc(scalar.dma_start(
                    out=XG_H,
                    in_=bass.AP(tensor=xsc_sum.ap().tensor,
                                offset=s0x + 128 * XCP * k,
                                ap=[[XCP, 128], [1, XCP]])), "sXgL", 16)

            @block.scalar
            def _(scalar, k=k):
                scalar.wait_ge(sems["sXgL"], 16 * (k + 1))
                vcol = XG_H[:, 1025:1026]
                scalar.activation(tcol[:, 0:1], vcol, AF.Ln, bias=EPS_LN)
                scalar.drain()
                scalar.activation(rhoc[:, k:k + 1], tcol[:, 0:1], AF.Exp, scale=-0.5)
                scalar.activation(rhoc[:, 4 + k:5 + k], tcol[:, 0:1], AF.Exp,
                                  scale=-1.0)
                inc(scalar.drain(), "eXgP", 1)

            @block.vector
            def _(vector, k=k):
                vector.wait_ge(sems["eXgP"], 12 * k + 1)
                vcol = XG_H[:, 1025:1026]
                vector.tensor_mul(tcol[:, 0:1], vcol, rhoc[:, 4 + k:5 + k])
                vector.drain()
                vector.tensor_scalar_mul(tcol[:, 0:1], tcol[:, 0:1], float(Dm))
                vector.drain()
                vector.tensor_scalar_max(rhoc[:, 8 + k:9 + k], tcol[:, 0:1],
                                         1e-24)
                inc(vector.drain(), "eXgP", 1)

            @block.scalar
            def _(scalar, k=k):
                scalar.wait_ge(sems["eXgP"], 12 * k + 2)
                scalar.activation(tcol[:, 0:1], rhoc[:, 8 + k:9 + k], AF.Ln)
                scalar.drain()
                scalar.activation(rhoc[:, 12 + k:13 + k], tcol[:, 0:1], AF.Exp,
                                  scale=-0.5)
                inc(scalar.drain(), "eXgP", 1)

            @block.vector
            def _(vector, k=k):
                vector.wait_ge(sems["eXgP"], 12 * k + 3)
                xk = XG_H[:, 0:1024]
                vector.tensor_mul(tcol[:, 0:1], rhoc[:, k:k + 1],
                                  rhoc[:, 12 + k:13 + k])
                vector.drain()
                vector.tensor_scalar_mul(tcol[:, 0:1], tcol[:, 0:1], SQD)
                vector.drain()
                vector.tensor_scalar(xk, xk, XG_H[:, 1024:1025],
                                     tcol[:, 0:1], ALU.subtract, ALU.mult)
                inc(vector.drain(), "eXgP", 1)

            @block.tensor
            def _(tensor, k=k):
                tensor.wait_ge(sems["eXgP"], 12 * k + 4)
                for j in range(8):
                    idx = k * 8 + j
                    if idx >= 2:
                        tensor.wait_ge(sems["eXgT"], idx - 1)
                    inc(tensor.transpose(
                        pT[idx % 2], XG_H[:, 128 * j:128 * (j + 1)], IDN),
                        "eXgP", 1)

            @block.scalar
            def _(scalar, k=k):
                for j in range(8):
                    idx = k * 8 + j
                    scalar.wait_ge(sems["eXgP"], 12 * k + 4 + j + 1)
                    scalar.activation(
                        XGN[:, 256 * j + 128 * k:256 * j + 128 * (k + 1)],
                        pT[idx % 2], AF.Copy, scale=vec_sb[:, 68 + j:69 + j])
                    inc(scalar.drain(), "eXgT", 1)

        @block.sync
        def _(sync):
            sync.wait_ge(sems["ccAR"], 4)
            for m in range(32):
                if m >= 1:
                    sync.wait_ge(sems["eH1m"], 8 * m)
                inc(sync.dma_start(
                    out=big5[:, 0:1024].rearrange("p (k d) -> p k d", k=8),
                    in_=w1hb[1024 * m:1024 * (m + 1), :]
                        .rearrange("(k p) d -> p k d", p=128)), "sW1L", 16)

        for m in range(32):
            pl = pD0 if m % 2 == 0 else pD1

            @block.tensor
            def _(tensor, m=m, pl=pl):
                tensor.wait_ge(sems["sW1L"], 16 * (m + 1))
                tensor.wait_ge(sems["eXgT"], 16)
                if m >= 2:
                    tensor.wait_ge(sems["eGelH"], m - 1)
                for k in range(8):
                    inc(tensor.matmul(
                        pl[:, 0:256],
                        big5[:, 128 * k:128 * (k + 1)],
                        XGN[:, 256 * k:256 * (k + 1)],
                        start=(k == 0), stop=(k == 7)), "eH1m", 1)

            @block.scalar
            def _(scalar, m=m, pl=pl):
                scalar.wait_ge(sems["eH1m"], 8 * (m + 1))
                scalar.activation(big1[:, 256 * m:256 * (m + 1)], pl[:, 0:256],
                                  AF.Gelu, bias=vec_sb[:, 28 + m:29 + m])
                inc(scalar.drain(), "eGelH", 1)
        H1M_END = cnt["eH1m"]

        @block.sync
        def _(sync):
            for m in range(8):
                for hf in range(2):
                    if m + hf >= 1:
                        sync.wait_ge(sems["eH2m"], 32 * m + 16 * hf)
                    inc(sync.dma_start(
                        out=big5[:, 1024:3072].rearrange("p (k d) -> p k d", k=16),
                        in_=w2hb[4096 * m + 2048 * hf:4096 * m + 2048 * (hf + 1), :]
                            .rearrange("(k p) d -> p k d", p=128)), "sW2L", 16)

        for m in range(8):
            pl = pL0 if m % 2 == 0 else pL1

            @block.tensor
            def _(tensor, m=m, pl=pl):
                tensor.wait_ge(sems["eGelH"], 32)
                if m >= 2:
                    tensor.wait_ge(sems["eH2e"], m - 1)
                for hf in range(2):
                    tensor.wait_ge(sems["sW2L"], 16 * (2 * m + hf + 1))
                    for kk in range(16):
                        k = 16 * hf + kk
                        inc(tensor.matmul(
                            pl[:, 0:256],
                            big5[:, 1024 + 128 * kk:1024 + 128 * (kk + 1)],
                            big1[:, 256 * k:256 * (k + 1)],
                            start=(k == 0), stop=(k == 31)), "eH2m", 1)

            @block.vector
            def _(vector, m=m, pl=pl):
                vector.wait_ge(sems["eH2m"], 32 * (m + 1))
                vector.tensor_scalar_add(scrH[:, 256 * (m % 2):256 * (m % 2) + 256],
                                         pl[:, 0:256],
                                         vec_sb[:, 60 + m:61 + m])
                inc(vector.drain(), "eH2e", 1)

            @block.tensor
            def _(tensor, m=m):
                tensor.wait_ge(sems["eH2e"], m + 1)
                for kk in range(2):
                    idx = 2 * m + kk
                    if idx >= 2:
                        tensor.wait_ge(sems["eH2T"], idx - 1)
                    inc(tensor.transpose(
                        pT[idx % 2],
                        scrH[:, 256 * (m % 2) + 128 * kk:256 * (m % 2) + 128 * (kk + 1)],
                        IDN), "eH1m", 1)

            @block.vector
            def _(vector, m=m):
                for kk in range(2):
                    idx = 2 * m + kk
                    vector.wait_ge(sems["eH1m"], H1M_END + idx + 1)
                    vector.tensor_copy(
                        h2tm[:, 1024 * kk + 128 * m:1024 * kk + 128 * (m + 1)],
                        pT[idx % 2])
                    inc(vector.drain(), "eH2T", 1)

        @block.sync
        def _(sync):
            sync.wait_ge(sems["eH2T"], 16)
            inc(sync.dma_start(
                out=ag_h2_in.ap().rearrange("(kk p) d -> p kk d", p=128),
                in_=h2tm.rearrange("p (kk d) -> p kk d", kk=2)), "sH2W", 16)

        @block.gpsimd
        def _(gpsimd):
            gpsimd.wait_ge(sems["sH2W"], 16)
            gpsimd.wait_ge(sems["sPadW"], 64)
            inc(gpsimd.collective_compute(
                "AllGather", ALU.bypass, replica_groups=[list(range(NCORE))],
                ins=[ag_h2_in.ap().opt()], outs=[h2full.ap()[0:NHG, :].opt()]),
                "ccH2", 1)

        @block.scalar
        def _(scalar):
            scalar.wait_ge(sems["ccH2"], 1)
            pv = scalar.value_load(pm1i[0:1, 0:1])
            inc(scalar.dma_start(
                out=H2OWN[:].rearrange("p (k f) -> p k f", k=4),
                in_=bass.AP(tensor=h2full.ap().tensor, offset=pv * Dm,
                            ap=[[Dm, 128], [128 * Dm, 4], [1, Dm]])), "sOwnL", 16)

        @block.tensor
        def _(tensor):
            tensor.wait_ge(sems["eSel"], 6)
            tensor.wait_ge(sems["eLmul"], 29)
            for q in range(4):
                pf = pF0 if q % 2 == 0 else pF1
                if q >= 2:
                    tensor.wait_ge(sems["eRelB"], q - 1)
                inc(tensor.matmul(pf[:, 0:512], ONE_ROW,
                                  rkrow[0:1, 512 * q:512 * (q + 1)],
                                  start=True, stop=True), "eYb", 1)

        @block.vector
        def _(vector):
            for q in range(4):
                pf = pF0 if q % 2 == 0 else pF1
                vector.wait_ge(sems["eYb"], 4 + q + 1)
                vector.tensor_copy(rowb[:, 512 * q:512 * (q + 1)], pf[:, 0:512])
                inc(vector.drain(), "eRelB", 1)

        @block.sync
        def _(sync):
            sync.wait_ge(sems["eH2m"], 8 * 32)
            for k in range(8):
                inc(sync.dma_start(out=big1[:, 1024 * k:1024 * (k + 1)],
                                   in_=woT[128 * k:128 * (k + 1), :]), "sFinW", 16)
            sync.wait_ge(sems["eLH1"], 16)
            for k in range(4):
                inc(sync.dma_start(out=big4[:, 1024 * k:1024 * (k + 1)],
                                   in_=w2lT[128 * k:128 * (k + 1), :]), "sFinW", 16)



        @block.sync
        def _(sync):
            sync.wait_ge(sems["eH1m"], 256)
            yoff = [sync.value_load(crow_sb[0:1, 4 + t:5 + t]) for t in range(4)]
            sync.wait_ge(sems["ccY"], 4)
            sync.wait_ge(sems["sH2W"], 16)
            for tt in range(4):
                if tt >= 1:
                    sync.wait_ge(sems["eFinM"], 16 * 8 * tt)
                inc(sync.dma_start(
                    out=ybig[:].rearrange("p (k f) -> p k f", k=8),
                    in_=bass.AP(tensor=agy_all.ap().tensor, offset=yoff[tt],
                                ap=[[4096, 128], [128 * 4096, 8], [1, 512]])),
                    "sFinY", 16)
                for dch in range(8):
                    g = tt * 8 + dch
                    inc(sync.dma_start(
                        out=xfin[:, 512 * (g % 2):512 * (g % 2 + 1)],
                        in_=x_fm[128 * dch:128 * (dch + 1),
                                 512 * tt:512 * (tt + 1)]), "sFinX", 16)
                    sync.wait_ge(sems["eFin"], g + 1)
                    inc(sync.dma_start(
                        out=out_ext[128 * dch:128 * (dch + 1),
                                    512 * tt:512 * (tt + 1)],
                        in_=outt[:, 512 * (g % 2):512 * (g % 2 + 1)]), "sOut", 16)

        for tt in range(4):

            @block.vector
            def _(vector, tt=tt):
                vector.wait_ge(sems["eRelB"], 4)
                if tt >= 1:
                    vector.wait_ge(sems["eFinM"], 16 * 8 * tt)
                for sc in range(4):
                    vector.tensor_scalar(Stt[:, 512 * sc:512 * (sc + 1)],
                                         rowb[:, 512 * tt:512 * (tt + 1)],
                                         IOTC[:, sc:sc + 1], None, ALU.is_equal)
                    inc(vector.drain(), "eStt", 1)

            for dch in range(8):
                g = tt * 8 + dch
                pf = pF0 if g % 2 == 0 else pF1

                @block.tensor
                def _(tensor, tt=tt, dch=dch, g=g, pf=pf):
                    tensor.wait_ge(sems["sFinW"], 16 * 12)
                    tensor.wait_ge(sems["sFinY"], 16 * (tt + 1))
                    tensor.wait_ge(sems["eStt"], 4 * (tt + 1))
                    tensor.wait_ge(sems["sOwnL"], 16)
                    if g == 0:
                        tensor.wait_ge(sems["eRelB"], 4)
                    if g >= 2:
                        tensor.wait_ge(sems["eFin"], g - 1)
                    for k in range(8):
                        inc(tensor.matmul(
                            pf[:, 0:512],
                            big1[:, 1024 * k + 128 * dch:1024 * k + 128 * (dch + 1)],
                            ybig[:, 512 * k:512 * (k + 1)],
                            start=(k == 0), stop=False), "eFinM", 1)
                    for k in range(4):
                        inc(tensor.matmul(
                            pf[:, 0:512],
                            big4[:, 1024 * k + 128 * dch:1024 * k + 128 * (dch + 1)],
                            big2[:, 2048 * k + 512 * tt:2048 * k + 512 * (tt + 1)],
                            start=False, stop=False), "eFinM", 1)
                    for k in range(4):
                        inc(tensor.matmul(
                            pf[:, 0:512],
                            H2OWN[:, 1024 * k + 128 * dch:1024 * k + 128 * (dch + 1)],
                            Stt[:, 512 * k:512 * (k + 1)],
                            start=False, stop=(k == 3)), "eFinM", 1)

                @block.vector
                def _(vector, tt=tt, dch=dch, g=g, pf=pf):
                    vector.wait_ge(sems["eFinM"], 16 * (g + 1))
                    vector.wait_ge(sems["sFinX"], 16 * (g + 1))
                    vector.scalar_tensor_tensor(
                        outt[:, 512 * (g % 2):512 * (g % 2 + 1)], pf[:, 0:512],
                        bstar[:, dch:dch + 1],
                        xfin[:, 512 * (g % 2):512 * (g % 2 + 1)],
                        ALU.add, ALU.add)
                    inc(vector.drain(), "eFin", 1)

        @block.sync
        def _(sync):
            sync.wait_ge(sems["sOut"], 16 * 32)

    _es.close()
    for cm in reversed(ctx):
        cm.__exit__(None, None, None)
    return nc


_NC_CACHE = None


def _get_nc():
    global _NC_CACHE
    if _NC_CACHE is None:
        _NC_CACHE = build_nc()
    return _NC_CACHE


def kernel(L0_attn_x, L0_attn_Wd, L0_attn_bd, L0_attn_a_log, L0_attn_b_param,
           L0_attn_Wo, L0_attn_bo, L0_ffn_routing_token, L0_ffn_gamma_l,
           L0_ffn_w1_l, L0_ffn_b1_l, L0_ffn_w2_l, L0_ffn_b2_l, L0_ffn_gamma_h,
           L0_ffn_w1_h, L0_ffn_b1_h, L0_ffn_w2_h, L0_ffn_b2_h):
    f = np.asarray
    x = np.ascontiguousarray(f(L0_attn_x, dtype=np.float32)).reshape(NT, Dm)

    def colsplit(v, nch):
        v = np.ascontiguousarray(f(v, dtype=np.float32)).reshape(nch, 128)
        return v.T

    wdT = np.ascontiguousarray(f(L0_attn_Wd, dtype=np.float32).T)
    woT = np.ascontiguousarray(f(L0_attn_Wo, dtype=np.float32).T)
    w1lT = np.ascontiguousarray(f(L0_ffn_w1_l, dtype=np.float32).T)
    w2lT = np.ascontiguousarray(f(L0_ffn_w2_l, dtype=np.float32).T)
    w1hT = f(L0_ffn_w1_h, dtype=np.float32).T
    w2hT = f(L0_ffn_w2_h, dtype=np.float32).T
    w1hb = np.ascontiguousarray(
        w1hT.reshape(Dm, 32, 128).transpose(1, 0, 2)).reshape(32 * Dm, 128)
    w2hb = np.ascontiguousarray(
        w2hT.reshape(4096, 8, 128).transpose(1, 0, 2)).reshape(8 * 4096, 128)

    vecs = np.zeros((128, 92), np.float32)
    vecs[:, 0:8] = colsplit(L0_attn_bd, 8)
    vecs[:, 8:16] = colsplit(L0_attn_bo, 8)
    vecs[:, 16:20] = colsplit(L0_ffn_b1_l, 4)
    vecs[:, 20:28] = colsplit(L0_ffn_b2_l, 8)
    vecs[:, 28:60] = colsplit(L0_ffn_b1_h, 32)
    vecs[:, 60:68] = colsplit(L0_ffn_b2_h, 8)
    vecs[:, 68:76] = colsplit(L0_ffn_gamma_h, 8)
    vecs[:, 76:84] = colsplit(L0_ffn_routing_token, 8)
    vecs[:, 84:92] = colsplit(L0_ffn_gamma_l, 8)

    ones = np.ones((128, 128), np.float32)
    lstr = (np.arange(128)[:, None] < np.arange(128)[None, :]).astype(np.float32)
    idn = np.eye(128, dtype=np.float32)
    iotc = (np.arange(128)[:, None] + 128 * np.arange(4)[None, :]).astype(np.float32)
    fco = np.zeros((128, 2), np.float32)
    fco[:, 0] = EPS_LN
    fco[:, 1] = -1e-4
    consts = np.concatenate([ones, lstr, idn, iotc, fco], axis=1)

    a_log = f(L0_attn_a_log, dtype=np.float32)
    b_par = f(L0_attn_b_param, dtype=np.float32)

    in_maps = []
    for c in range(NCORE):
        g0 = TOK * c
        apbc = np.concatenate([a_log[128 * c:128 * (c + 1)],
                               b_par[128 * c:128 * (c + 1)]], axis=1)
        crow_v = np.zeros((1, 8), np.int32)
        crow_v[0, 0] = c
        crow_v[0, 1] = 8388608 * (c // 4) + 262144 * (c % 4)
        crow_v[0, 2] = g0
        crow_v[0, 3] = SLOT * c * XCP
        for t in range(4):
            gt = g0 + 512 * t
            q = gt // 4096
            col = gt % 4096
            crow_v[0, 4 + t] = (1024 * q) * 4096 + col
        in_maps.append(dict(
            x_tm=np.ascontiguousarray(x[g0:g0 + TOK]),
            x_fm=np.ascontiguousarray(x[g0:g0 + TOK].T),
            wdT=wdT, woT=woT, w1lT=w1lT, w2lT=w2lT, w1hb=w1hb, w2hb=w2hb,
            vecs=vecs, apb=np.ascontiguousarray(apbc), consts=consts, crow=crow_v,
        ))

    nc = _get_nc()
    res = run_bass_kernel_spmd(nc, in_maps, core_ids=list(range(NCORE)))
    out = np.empty((Bsz, Tseq, Dm), np.float32)
    for c in range(NCORE):
        o = res.results[c]["out"]
        b, t0 = c // 4, (c % 4) * TOK
        out[b, t0:t0 + TOK] = o.T
    return out


# version 07
# speedup vs baseline: 1.0102x; 1.0102x over previous
import numpy as np
import concourse.bass as bass
import concourse.mybir as mybir
from concourse.bass_utils import run_bass_kernel_spmd
from concourse.bass_core_rust import IndirectOffsetOnAxis

F32 = mybir.dt.float32
I32 = mybir.dt.int32
AF = mybir.ActivationFunctionType
ALU = mybir.AluOpType
AX = mybir.AxisListType

NCORE = 8
Bsz = 2
Tseq = 8192
Dm = 1024
Sst = 16
TOK = 2048
NT = Bsz * Tseq
KH = 1024
NHG = 2048
KEFF = 1152.0
NIT = 50
SLOT = 256
WIN = 512
XCP = 1028
EPS_LN = 1e-5
BIGRANK = 9999.0
SQD = float(np.sqrt(Dm))

A1 = float(np.float32(np.float32(np.log(np.float32(KEFF)))
                      - np.float32(np.log(np.float32(8192.0)))))
NLOGK = float(-np.float32(np.log(np.float32(KEFF))))


def build_nc():
    nc = bass.Bass("TRN2", target_bir_lowering=False, debug=False,
                   detect_race_conditions=False)

    x_tm = nc.dram_tensor("x_tm", [TOK, Dm], F32, kind="ExternalInput").ap()
    x_fm = nc.dram_tensor("x_fm", [Dm, TOK], F32, kind="ExternalInput").ap()
    wdT = nc.dram_tensor("wdT", [Dm, Dm], F32, kind="ExternalInput").ap()
    woT = nc.dram_tensor("woT", [Dm, Dm], F32, kind="ExternalInput").ap()
    w1lT = nc.dram_tensor("w1lT", [Dm, 512], F32, kind="ExternalInput").ap()
    w2lT = nc.dram_tensor("w2lT", [512, Dm], F32, kind="ExternalInput").ap()
    w1hb = nc.dram_tensor("w1hb", [32 * Dm, 128], F32, kind="ExternalInput").ap()
    w2hb = nc.dram_tensor("w2hb", [8 * 4096, 128], F32, kind="ExternalInput").ap()
    vecs = nc.dram_tensor("vecs", [128, 92], F32, kind="ExternalInput").ap()
    apb = nc.dram_tensor("apb", [128, 32], F32, kind="ExternalInput").ap()
    consts = nc.dram_tensor("consts", [128, 390], F32, kind="ExternalInput").ap()
    crow = nc.dram_tensor("crow", [1, 8], I32, kind="ExternalInput").ap()

    out_ext = nc.dram_tensor("out", [Dm, TOK], F32, kind="ExternalOutput").ap()

    ag_s_in = nc.dram_tensor("ag_s_in", [1, TOK], F32)
    ag_s_out = nc.dram_tensor("ag_s_out", [8, TOK], F32)
    agd_in = nc.dram_tensor("agd_in", [Dm, TOK], F32)
    agu_in = nc.dram_tensor("agu_in", [Dm, TOK], F32)
    agd_out = nc.dram_tensor("agd_out", [2 * 8 * 512, TOK], F32)
    agu_out = nc.dram_tensor("agu_out", [2 * 8 * 512, TOK], F32)
    agy_in = nc.dram_tensor("agy_in", [512, 4096], F32)
    agy_all = nc.dram_tensor("agy_all", [4 * 1024, 4096], F32)
    rank_dram = nc.dram_tensor("rank_dram", [Bsz, Tseq], F32)
    rk_loc = nc.dram_tensor("rk_loc", [1, TOK], F32)
    d_loc = nc.dram_tensor("d_loc", [8 * 128, TOK], F32)
    u_loc = nc.dram_tensor("u_loc", [8 * 128, TOK], F32)
    xsc = nc.dram_tensor("xsc", [NHG, XCP], F32)
    xsc_sum = nc.dram_tensor("xsc_sum", [NHG, XCP], F32)
    ag_h2_in = nc.dram_tensor("ag_h2_in", [SLOT, Dm], F32)
    h2full = nc.dram_tensor("h2full", [NHG + WIN, Dm], F32)

    sems = {}
    cnt = {}
    ctx = []

    def sem(name):
        cm = nc.semaphore(name)
        sems[name] = cm.__enter__()
        ctx.append(cm)
        cnt[name] = 0

    for n in ["sLdA", "sLdX", "sDW0", "sDW1", "sSw", "sScanL0", "sScanL1",
              "sYW0", "sYW1", "sYW2", "sYW3", "sMisc", "sZero", "sXrl",
              "sXscW", "sRkL", "sRflR", "sRfl0", "sRfl1",
              "sXgL", "sW1L", "sW2L", "sOwnL", "sFinW", "sFinY", "sFinX",
              "sOut", "sRhoL", "sH2W", "sPadW", "sRfl",
              "eStat", "eVar", "eXn", "eTrans", "eEvacXn", "eDelta", "eU",
              "eRho", "eLH1", "eLmul", "eSrow", "eSq", "rtd", "rta", "rtp",
              "btd", "bta", "btp", "eSel", "eRelB", "eExp", "eScan", "eSup",
              "eXgP", "eXgT", "eH1m", "eGelH", "eH2m", "eH2e", "eH2T",
              "eFinM", "eFin", "eYb", "eStt",
              "ccS", "ccDU", "ccY", "ccAR", "ccH2"]:
        sem(n)

    def inc(instr, name, amt):
        cnt[name] += amt
        instr.then_inc(sems[name], amt)
        return cnt[name]

    import contextlib
    _es = contextlib.ExitStack()
    big0 = _es.enter_context(nc.sbuf_tensor("big0", [128, 16384], F32))
    big1 = _es.enter_context(nc.sbuf_tensor("big1", [128, 8192], F32))
    big2 = _es.enter_context(nc.sbuf_tensor("big2", [128, 8192], F32))
    big4 = _es.enter_context(nc.sbuf_tensor("big4", [128, 4096], F32))
    big5 = _es.enter_context(nc.sbuf_tensor("big5", [128, 3072], F32))
    big6 = _es.enter_context(nc.sbuf_tensor("big6", [128, 3076], F32))
    ybig = _es.enter_context(nc.sbuf_tensor("ybig", [128, 4096], F32))
    xch = _es.enter_context(nc.sbuf_tensor("xch", [128, 1024], F32))
    rowb = _es.enter_context(nc.sbuf_tensor("rowb", [128, 2048], F32))
    Stt = _es.enter_context(nc.sbuf_tensor("Stt", [128, 2048], F32))
    mv = _es.enter_context(nc.sbuf_tensor("mv", [128, 32], F32))
    rhoc = _es.enter_context(nc.sbuf_tensor("rhoc", [128, 16], F32))
    vec_sb = _es.enter_context(nc.sbuf_tensor("vec_sb", [128, 92], F32))
    bstar = _es.enter_context(nc.sbuf_tensor("bstar", [128, 8], F32))
    apb_sb = _es.enter_context(nc.sbuf_tensor("apb_sb", [128, 32], F32))
    napos = _es.enter_context(nc.sbuf_tensor("napos", [128, 16], F32))
    cst = _es.enter_context(nc.sbuf_tensor("cst", [128, 390], F32))
    crow_sb = _es.enter_context(nc.sbuf_tensor("crow_sb", [1, 8], I32))
    s128 = _es.enter_context(nc.sbuf_tensor("s128", [128, 128], F32))
    w128 = _es.enter_context(nc.sbuf_tensor("w128", [128, 128], F32))
    m2 = _es.enter_context(nc.sbuf_tensor("m2", [128, 2], F32))
    na2 = _es.enter_context(nc.sbuf_tensor("na2", [128, 2], F32))
    mm2 = _es.enter_context(nc.sbuf_tensor("mm2", [128, 2], F32))
    t2a = _es.enter_context(nc.sbuf_tensor("t2a", [128, 2], F32))
    t2b = _es.enter_context(nc.sbuf_tensor("t2b", [128, 2], F32))
    rs2 = _es.enter_context(nc.sbuf_tensor("rs2", [128, 2], F32))
    tcol = _es.enter_context(nc.sbuf_tensor("tcol", [128, 4], F32))
    i2a = _es.enter_context(nc.sbuf_tensor("i2a", [128, 2], I32))
    i2b = _es.enter_context(nc.sbuf_tensor("i2b", [128, 2], I32))
    i2c = _es.enter_context(nc.sbuf_tensor("i2c", [128, 2], I32))
    i2d = _es.enter_context(nc.sbuf_tensor("i2d", [128, 2], I32))
    i2e = _es.enter_context(nc.sbuf_tensor("i2e", [128, 2], I32))
    thk = _es.enter_context(nc.sbuf_tensor("thk", [128, 2], I32))
    tt2 = _es.enter_context(nc.sbuf_tensor("tt2", [128, 2], F32))
    sc2 = _es.enter_context(nc.sbuf_tensor("sc2", [128, 2], F32))
    pD0 = _es.enter_context(nc.psum_tensor("pD0", [128, 512], F32))
    pD1 = _es.enter_context(nc.psum_tensor("pD1", [128, 512], F32))
    pL0 = _es.enter_context(nc.psum_tensor("pL0", [128, 512], F32))
    pL1 = _es.enter_context(nc.psum_tensor("pL1", [128, 512], F32))
    pF0 = _es.enter_context(nc.psum_tensor("pF0", [128, 512], F32))
    pF1 = _es.enter_context(nc.psum_tensor("pF1", [128, 512], F32))
    pTA = _es.enter_context(nc.psum_tensor("pTA", [128, 512], F32))
    pTB = _es.enter_context(nc.psum_tensor("pTB", [128, 512], F32))
    block = _es.enter_context(nc.Block())
    zrow = big6[:, 0:1028]
    XG_H = big6[:, 0:1028]
    XGN = big6[:, 1028:3076]
    scrC = big6[:, 1028:1540]
    scrH = big6[:, 1028:1540]
    xfin = big6[:, 0:1024]
    outt = big6[:, 1028:2052]
    dlu = ybig[:, 0:2048]
    h2tm = ybig[:, 2048:4096]
    rolrow = ybig[0:1, 2048:4096]
    scrB = Stt[:, 0:1024]
    sev = Stt[0:1, 1024:1536]
    rkrow = Stt[0:1, 0:2048]
    scrR = rowb[:, 0:80]
    t16 = w128[0:16, 0:128]
    e128 = Stt[:, 1024:1152]
    v128 = Stt[:, 1152:1280]
    tr2 = Stt[0:2, 1280:1408]
    ms1 = Stt[0:1, 1408:1410]
    pm1 = tcol[0:1, 1:2]
    pm1i = tcol[0:1, 2:3].bitcast(I32)
    rkc = ybig[:, 4:20]
    rkci = ybig[:, 20:36].bitcast(I32)
    state = ybig[:, 36:52]
    if True:
        pT = [pTA[:, 0:128], pTB[:, 0:128]]
        pR = pTA[:, 384:386]
        pS = pTB[0:1, 256:768] if False else pTB[0:1, 0:512]

        ONES = cst[:, 0:128]
        ONE_ROW = cst[0:1, 0:128]
        LSTR = cst[:, 128:256]
        IDN = cst[:, 256:384]
        IOTC = cst[:, 384:388]
        nc.const_aps.register(F32, EPS_LN, cst[:, 388:389])
        nc.const_aps.register(F32, -1e-4, cst[:, 389:390])
        DLT = [big0[:, 0:1024], big0[:, 1024:2048]]
        USB = [big0[:, 2048:3072], big0[:, 3072:4096]]
        ABR = [big0[:, 4096:5120], big0[:, 5120:6144]]
        GSB = [big0[:, 6144:7168], big0[:, 7168:8192]]
        YSB = [big0[:, 8192:9216], big0[:, 9216:10240]]
        H2OWN = big0[:, 10240:14336]
        s3 = s128[:].rearrange("p (b f) -> p b f", b=2)
        v3 = v128[:].rearrange("p (b f) -> p b f", b=2)
        e3 = e128[:].rearrange("p (b f) -> p b f", b=2)
        w3 = w128[:].rearrange("p (b f) -> p b f", b=2)
        key3 = w128[:].bitcast(I32).rearrange("p (b f) -> p b f", b=2)

        def bc2(ap2):
            return ap2.unsqueeze(2).to_broadcast([128, 2, 64])

        @block.sync
        def _(sync):
            inc(sync.dma_start(out=cst[:], in_=consts), "sLdA", 16)
            inc(sync.dma_start(out=vec_sb[:], in_=vecs), "sLdA", 16)
            inc(sync.dma_start(out=apb_sb[:], in_=apb), "sLdA", 16)
            inc(sync.dma_start(out=crow_sb[:], in_=crow), "sLdA", 16)
            for k in range(8):
                inc(sync.dma_start(out=big1[:, 1024 * k:1024 * (k + 1)],
                                   in_=wdT[128 * k:128 * (k + 1), :]), "sLdA", 16)
            for k in range(8):
                inc(sync.dma_start(out=big4[:, 512 * k:512 * (k + 1)],
                                   in_=w1lT[128 * k:128 * (k + 1), :]), "sLdA", 16)
            for ch in range(16):
                if ch >= 1:
                    sync.wait_ge(sems["eTrans"], 8 * ch)
                inc(sync.dma_start(out=xch[:],
                                   in_=x_tm[128 * ch:128 * (ch + 1), :]), "sLdX", 16)
        LDA_X = [16 * (ch + 1) for ch in range(16)]

        @block.scalar
        def _(scalar):
            scalar.wait_ge(sems["sLdA"], 16 * 20)
            scalar.activation(scrB[:, 0:16], apb_sb[:, 0:16], AF.Exp)
            scalar.drain()
            scalar.activation(scrB[:, 16:32], scrB[:, 0:16], AF.Ln, bias=1.0)
            scalar.drain()
            scalar.activation(napos[:], scrB[:, 16:32], AF.Copy,
                              bias=-1e-4, scale=-1.0)
            inc(scalar.drain(), "eRho", 1)

        @block.vector
        def _(vector):
            vector.wait_ge(sems["sLdA"], 16 * 20)
            vector.memset(zrow, 0.0)
            vector.memset(state[:], 0.0)
            inc(vector.tensor_add(bstar[:], vec_sb[:, 8:16], vec_sb[:, 20:28]),
                "eStat", 1)

        for ch in range(16):
            sl = xch[:]

            @block.vector
            def _(vector, ch=ch, sl=sl):
                vector.wait_ge(sems["sLdX"], LDA_X[ch])
                vector.reduce_sum(mv[:, 2 * ch:2 * ch + 1], sl, AX.X)
                inc(vector.drain(), "eStat", 1)

            @block.scalar
            def _(scalar, ch=ch, sl=sl):
                scalar.wait_ge(sems["sLdX"], LDA_X[ch])
                scalar.activation(big5[:, 0:1024], sl, AF.Square)
                inc(scalar.drain(), "eSq", 1)

            @block.vector
            def _(vector, ch=ch):
                vector.wait_ge(sems["eSq"], ch + 1)
                vector.reduce_sum(mv[:, 2 * ch + 1:2 * ch + 2],
                                  big5[:, 0:1024], AX.X)
                inc(vector.drain(), "eStat", 1)

            @block.vector
            def _(vector, ch=ch):
                vector.wait_ge(sems["eStat"], 1 + 2 * (ch + 1))
                m_ = mv[:, 2 * ch:2 * ch + 1]
                v_ = mv[:, 2 * ch + 1:2 * ch + 2]
                vector.tensor_scalar_mul(m_, m_, 1.0 / Dm)
                vector.tensor_scalar_mul(v_, v_, 1.0 / Dm)
                vector.drain()
                vector.tensor_mul(tcol[:, 0:1], m_, m_)
                vector.drain()
                vector.tensor_sub(v_, v_, tcol[:, 0:1])
                inc(vector.drain(), "eVar", 1)

            @block.scalar
            def _(scalar, ch=ch):
                scalar.wait_ge(sems["eVar"], ch + 1)
                scalar.activation(tcol[:, 1:2], mv[:, 2 * ch + 1:2 * ch + 2],
                                  AF.Ln, bias=EPS_LN)
                scalar.drain()
                scalar.activation(rhoc[:, ch:ch + 1], tcol[:, 1:2], AF.Exp,
                                  scale=-0.5)
                inc(scalar.drain(), "eRho", 1)

            @block.vector
            def _(vector, ch=ch, sl=sl):
                vector.wait_ge(sems["eRho"], 1 + ch + 1)
                vector.tensor_scalar(sl, sl, mv[:, 2 * ch:2 * ch + 1],
                                     rhoc[:, ch:ch + 1], ALU.subtract, ALU.mult)
                inc(vector.drain(), "eXn", 1)

            @block.tensor
            def _(tensor, ch=ch, sl=sl):
                tensor.wait_ge(sems["eXn"], ch + 1)
                for j in range(8):
                    idx = ch * 8 + j
                    if idx >= 2:
                        tensor.wait_ge(sems["eEvacXn"], idx - 1)
                    inc(tensor.transpose(pT[idx % 2], sl[:, 128 * j:128 * (j + 1)],
                                         IDN), "eTrans", 1)

            @block.scalar
            def _(scalar, ch=ch):
                for j in range(8):
                    idx = ch * 8 + j
                    scalar.wait_ge(sems["eTrans"], idx + 1)
                    scalar.copy(
                        big0[:, 2048 * j + 128 * ch:2048 * j + 128 * (ch + 1)],
                        pT[idx % 2])
                    inc(scalar.drain(), "eEvacXn", 1)

        @block.scalar
        def _(scalar):
            scalar.wait_ge(sems["eVar"], 16)
            vcols = mv[:].rearrange("p (c two) -> p two c", two=2)[:, 1, :]
            scalar.activation(scrR[:, 0:16], vcols, AF.Ln, bias=EPS_LN)
            scalar.drain()
            scalar.activation(scrR[:, 16:32], scrR[:, 0:16], AF.Exp, scale=-1.0)
            inc(scalar.drain(), "eRho", 1)

        @block.vector
        def _(vector):
            vector.wait_ge(sems["eRho"], 18)
            vcols = mv[:].rearrange("p (c two) -> p two c", two=2)[:, 1, :]
            vector.tensor_mul(scrR[:, 32:48], vcols, scrR[:, 16:32])
            vector.drain()
            vector.tensor_scalar_mul(scrR[:, 32:48], scrR[:, 32:48], float(Dm))
            vector.drain()
            vector.tensor_scalar_max(scrR[:, 32:48], scrR[:, 32:48], 1e-24)
            inc(vector.drain(), "eLmul", 1)

        @block.scalar
        def _(scalar):
            scalar.wait_ge(sems["eLmul"], 1)
            scalar.activation(scrR[:, 48:64], scrR[:, 32:48], AF.Ln)
            scalar.drain()
            scalar.activation(scrR[:, 64:80], scrR[:, 48:64], AF.Exp, scale=-0.5)
            inc(scalar.drain(), "eRho", 1)

        @block.tensor
        def _(tensor):
            tensor.wait_ge(sems["eRho"], 19)
            tensor.wait_ge(sems["eEvacXn"], 127)
            inc(tensor.transpose(pTA[0:16, 0:128], scrR[:, 64:80], IDN), "eTrans", 1)
        TR_RHO = cnt["eTrans"]

        @block.scalar
        def _(scalar):
            scalar.wait_ge(sems["eTrans"], TR_RHO)
            scalar.copy(t16, pTA[0:16, 0:128])
            inc(scalar.drain(), "eEvacXn", 1)
        EV_RHO = cnt["eEvacXn"]

        @block.scalar
        def _(scalar):
            scalar.wait_ge(sems["eEvacXn"], EV_RHO)
            for ch in range(16):
                amt = inc(scalar.dma_start(out=rolrow[0:1, 128 * ch:128 * (ch + 1)],
                                           in_=t16[ch:ch + 1, 0:128]), "sRhoL", 16)

        @block.tensor
        def _(tensor):
            tensor.wait_ge(sems["sRhoL"], 256)
            for q in range(4):
                pf = pF0 if q % 2 == 0 else pF1
                if q >= 2:
                    tensor.wait_ge(sems["eLmul"], q)
                inc(tensor.matmul(pf[:, 0:512], ONE_ROW,
                                  rolrow[0:1, 512 * q:512 * (q + 1)],
                                  start=True, stop=True), "eYb", 1)

        @block.vector
        def _(vector):
            for q in range(4):
                pf = pF0 if q % 2 == 0 else pF1
                vector.wait_ge(sems["eYb"], q + 1)
                vector.tensor_copy(rowb[:, 512 * q:512 * (q + 1)], pf[:, 0:512])
                inc(vector.drain(), "eLmul", 1)

        @block.vector
        def _(vector):
            vector.wait_ge(sems["sLdA"], 16 * 20)
            for k in range(8):
                vector.tensor_scalar_mul(tcol[:, 2:3], vec_sb[:, 84 + k:85 + k], SQD)
                vector.drain()
                vector.tensor_scalar_mul(big4[:, 512 * k:512 * (k + 1)],
                                         big4[:, 512 * k:512 * (k + 1)],
                                         tcol[:, 2:3])
                inc(vector.drain(), "eLmul", 1)

        TR_S0 = cnt["eTrans"]
        for tt in range(4):

            @block.tensor
            def _(tensor, tt=tt):
                if tt == 0:
                    tensor.wait_ge(sems["eEvacXn"], 128)
                if tt >= 1:
                    tensor.wait_ge(sems["eSrow"], tt)
                for k in range(8):
                    inc(tensor.matmul(pS, vec_sb[:, 76 + k:77 + k],
                                      big0[:, 2048 * k + 512 * tt:2048 * k + 512 * (tt + 1)],
                                      start=(k == 0), stop=(k == 7)), "eTrans", 1)

            @block.vector
            def _(vector, tt=tt):
                vector.wait_ge(sems["eTrans"], TR_S0 + 8 * (tt + 1))
                if tt >= 1:
                    vector.wait_ge(sems["sSw"], 16 * tt)
                vector.tensor_copy(sev, pS)
                inc(vector.drain(), "eSrow", 1)

            @block.sync
            def _(sync, tt=tt):
                sync.wait_ge(sems["eSrow"], tt + 1)
                inc(sync.dma_start(out=ag_s_in.ap()[0:1, 512 * tt:512 * (tt + 1)],
                                   in_=sev), "sSw", 16)

        @block.sync
        def _(sync):
            sync.wait_ge(sems["ccS"], 1)
            for b in range(Bsz):
                inc(sync.dma_start(
                    out=s128[:, 64 * b:64 * (b + 1)],
                    in_=ag_s_out.ap()[4 * b:4 * b + 4, :]
                        .rearrange("r t -> (r t)").rearrange("(p f) -> p f", p=128)),
                    "sSw", 16)

        TR_D0 = cnt["eTrans"]

        def emit_delta(g):
            if True:
                ech, tt = g // 4, g % 4
                pd = pD0 if g % 2 == 0 else pD1

                @block.tensor
                def _(tensor, tt=tt, ech=ech, g=g, pd=pd):
                    tensor.wait_ge(sems["eEvacXn"], 8 * (4 * tt + 4))
                    if g >= 2:
                        tensor.wait_ge(sems["eDelta"], g - 1)
                    for k in range(8):
                        inc(tensor.matmul(
                            pd[:, 0:512],
                            big1[:, 1024 * k + 128 * ech:1024 * k + 128 * (ech + 1)],
                            big0[:, 2048 * k + 512 * tt:2048 * k + 512 * (tt + 1)],
                            start=(k == 0), stop=(k == 7)), "eTrans", 1)

                @block.scalar
                def _(scalar, g=g, ech=ech, pd=pd):
                    scalar.wait_ge(sems["eTrans"], TR_D0 + 8 * (g + 1))
                    if g >= 2:
                        scalar.wait_ge(sems["eU"], g - 1)
                        scalar.wait_ge(sems["sDW0" if g % 2 == 0 else "sDW1"],
                                       32 * ((g - 2) // 2 + 1))
                    dl = dlu[:, 1024 * (g % 2):1024 * (g % 2) + 512]
                    scalar.activation(scrB[:, 0:512], pd[:, 0:512], AF.Exp,
                                      bias=vec_sb[:, ech:ech + 1])
                    scalar.drain()
                    scalar.activation(dl, scrB[:, 0:512], AF.Ln, bias=1.0)
                    inc(scalar.drain(), "eDelta", 1)

                @block.vector
                def _(vector, tt=tt, ech=ech, g=g):
                    dl = dlu[:, 1024 * (g % 2):1024 * (g % 2) + 512]
                    ut = dlu[:, 1024 * (g % 2) + 512:1024 * (g % 2) + 1024]
                    vector.wait_ge(sems["eDelta"], g + 1)
                    vector.tensor_mul(
                        ut, dl,
                        big0[:, 2048 * ech + 512 * tt:2048 * ech + 512 * (tt + 1)])
                    inc(vector.drain(), "eU", 1)

        @block.sync
        def _(sync):
            for ech in range(8):
                for tt in range(4):
                    g = ech * 4 + tt
                    dl = dlu[:, 1024 * (g % 2):1024 * (g % 2) + 512]
                    ut = dlu[:, 1024 * (g % 2) + 512:1024 * (g % 2) + 1024]
                    sync.wait_ge(sems["eU"], g + 1)
                    par = "sDW0" if g % 2 == 0 else "sDW1"
                    inc(sync.dma_start(
                        out=agd_in.ap()[128 * ech:128 * (ech + 1),
                                        512 * tt:512 * (tt + 1)], in_=dl), par, 16)
                    inc(sync.dma_start(
                        out=agu_in.ap()[128 * ech:128 * (ech + 1),
                                        512 * tt:512 * (tt + 1)], in_=ut), par, 16)

        @block.gpsimd
        def _(gpsimd):
            gpsimd.wait_ge(sems["sSw"], 64)
            inc(gpsimd.collective_compute(
                "AllGather", ALU.bypass, replica_groups=[list(range(NCORE))],
                ins=[ag_s_in.ap().opt()], outs=[ag_s_out.ap().opt()]), "ccS", 1)
            for h in range(2):
                gpsimd.wait_ge(sems["sDW0"], 256 if h == 0 else 512)
                gpsimd.wait_ge(sems["sDW1"], 256 if h == 0 else 512)
                inc(gpsimd.collective_compute(
                    "AllGather", ALU.bypass, replica_groups=[list(range(NCORE))],
                    ins=[agd_in.ap()[512 * h:512 * (h + 1), :].opt()],
                    outs=[agd_out.ap()[4096 * h:4096 * (h + 1), :].opt()]), "ccDU", 1)
                inc(gpsimd.collective_compute(
                    "AllGather", ALU.bypass, replica_groups=[list(range(NCORE))],
                    ins=[agu_in.ap()[512 * h:512 * (h + 1), :].opt()],
                    outs=[agu_out.ap()[4096 * h:4096 * (h + 1), :].opt()]), "ccDU", 1)

        @block.vector
        def _(vector):
            vector.wait_ge(sems["sSw"], 96)
            for b in range(Bsz):
                vector.reduce_max(t2a[:, b:b + 1], s128[:, 64 * b:64 * (b + 1)], AX.X)
            vector.drain()
            vector.tensor_copy(t2b[:, 0:2], t2a[:, 0:2])
            inc(vector.drain(), "rtd", 1)

        @block.tensor
        def _(tensor):
            tensor.wait_ge(sems["rtd"], 1)
            tensor.wait_ge(sems["eEvacXn"], EV_RHO)
            inc(tensor.transpose(pTA[0:2, 0:128], t2b[:, 0:2], IDN), "rtp", 1)

        @block.vector
        def _(vector):
            vector.wait_ge(sems["rtp"], 1)
            vector.tensor_copy(tr2[:, 0:128], pTA[0:2, 0:128])
            vector.drain()
            vector.reduce_max(t2a[0:2, 0:1], tr2[:, 0:128], AX.X)
            inc(vector.drain(), "rtd", 1)

        @block.tensor
        def _(tensor):
            tensor.wait_ge(sems["rtd"], 2)
            inc(tensor.transpose(pTB[0:1, 0:2], t2a[0:2, 0:1], IDN[0:2, 0:2]), "rtp", 1)

        @block.vector
        def _(vector):
            vector.wait_ge(sems["rtp"], 2)
            vector.tensor_copy(ms1[0:1, 0:2], pTB[0:1, 0:2])
            inc(vector.drain(), "rtd", 1)

        @block.tensor
        def _(tensor):
            tensor.wait_ge(sems["rtd"], 3)
            inc(tensor.matmul(pR, ONE_ROW, ms1[0:1, 0:2], start=True, stop=True),
                "rtp", 1)

        @block.vector
        def _(vector):
            vector.wait_ge(sems["rtp"], 3)
            vector.tensor_copy(m2[:, 0:2], pR)
            vector.memset(na2[:], -A1)
            vector.drain()
            vector.tensor_tensor(mm2[:], m2[:], na2[:], ALU.min)
            inc(vector.drain(), "rtd", 1)

        RTD0, RTA0, RTP0 = cnt["rtd"], cnt["rta"], cnt["rtp"]
        for j in range(NIT - 1):

            @block.vector
            def _(vector, j=j):
                if j == 0:
                    vector.wait_ge(sems["sRhoL"], 256)
                if j > 0:
                    vector.wait_ge(sems["rta"], RTA0 + 2 * j)
                    vector.scalar_tensor_tensor(na2[:], t2a[:, 0:2], NLOGK,
                                                mm2[:], ALU.add, ALU.add)
                    vector.drain()
                    vector.tensor_tensor(mm2[:], m2[:], na2[:], ALU.min)
                    vector.drain()
                vector.tensor_tensor(v3, s3, bc2(na2[:]), ALU.min)
                vector.drain()
                vector.tensor_tensor(w3, v3, bc2(mm2[:]), ALU.subtract)
                inc(vector.drain(), "rtd", 1)

            @block.scalar
            def _(scalar, j=j):
                scalar.wait_ge(sems["rtd"], RTD0 + 2 * j + 1)
                scalar.activation(e128[:], w128[:], AF.Exp)
                inc(scalar.drain(), "rta", 1)

            @block.vector
            def _(vector, j=j):
                vector.wait_ge(sems["rta"], RTA0 + 2 * j + 1)
                vector.reduce_sum(rs2[:, 0:2], e3, AX.X)
                inc(vector.drain(), "rtd", 1)

            @block.tensor
            def _(tensor, j=j):
                tensor.wait_ge(sems["rtd"], RTD0 + 2 * j + 2)
                if j > 0:
                    tensor.wait_ge(sems["rta"], RTA0 + 2 * j)
                inc(tensor.matmul(pR, ONES, rs2[:, 0:2], start=True, stop=True),
                    "rtp", 1)

            @block.scalar
            def _(scalar, j=j):
                scalar.wait_ge(sems["rtp"], RTP0 + j + 1)
                scalar.activation(t2a[:, 0:2], pR, AF.Ln)
                inc(scalar.drain(), "rta", 1)

            if j < 32:
                emit_delta(j)

        @block.vector
        def _(vector):
            vector.wait_ge(sems["rta"], RTA0 + 2 * (NIT - 1))
            vector.scalar_tensor_tensor(na2[:], t2a[:, 0:2], NLOGK, mm2[:],
                                        ALU.add, ALU.add)
            vector.drain()
            vector.tensor_tensor(v3, s3, bc2(na2[:]), ALU.subtract)
            vector.drain()
            vector.tensor_scalar_min(v128[:], v128[:], 0.0)
            vector.memset(w128[:], 0.0)
            vector.drain()
            vector.tensor_sub(w128[:], w128[:], v128[:])
            vector.memset(i2a[:], 0)
            vector.memset(i2b[:], 2139095039)
            inc(vector.drain(), "btd", 1)

        def bisect_hi_update(vector):
            vector.tensor_single_scalar(sc2[:], t2b[:, 0:2], float(KH), ALU.is_ge)
            vector.drain()
            vector.tensor_copy(i2e[:], sc2[:])
            vector.tensor_sub(i2d[:], i2c[:], i2b[:])
            vector.drain()
            vector.tensor_mul(i2d[:], i2d[:], i2e[:])
            vector.drain()
            vector.tensor_add(i2b[:], i2b[:], i2d[:])
            vector.drain()

        for j in range(31):

            @block.vector
            def _(vector, j=j):
                if j > 0:
                    vector.wait_ge(sems["bta"], j)
                    bisect_hi_update(vector)
                    vector.tensor_scalar(i2e[:], i2e[:], -1, 1, ALU.mult, ALU.add)
                    vector.tensor_sub(i2d[:], i2c[:], i2a[:])
                    vector.drain()
                    vector.tensor_scalar_add(i2d[:], i2d[:], 1)
                    vector.drain()
                    vector.tensor_mul(i2d[:], i2d[:], i2e[:])
                    vector.drain()
                    vector.tensor_add(i2a[:], i2a[:], i2d[:])
                    vector.drain()
                vector.tensor_sub(i2c[:], i2b[:], i2a[:])
                vector.drain()
                vector.tensor_single_scalar(i2c[:], i2c[:], 1, ALU.logical_shift_right)
                vector.drain()
                vector.tensor_add(i2c[:], i2c[:], i2a[:])
                vector.drain()
                vector.tensor_tensor(e3, key3, bc2(i2c[:]), ALU.is_le)
                vector.drain()
                vector.reduce_sum(rs2[:, 0:2], e3, AX.X)
                inc(vector.drain(), "btd", 1)

            @block.tensor
            def _(tensor, j=j):
                tensor.wait_ge(sems["btd"], j + 2)
                if j > 0:
                    tensor.wait_ge(sems["bta"], j)
                inc(tensor.matmul(pR, ONES, rs2[:, 0:2], start=True, stop=True),
                    "btp", 1)

            @block.scalar
            def _(scalar, j=j):
                scalar.wait_ge(sems["btp"], j + 1)
                scalar.copy(t2b[:, 0:2], pR)
                inc(scalar.drain(), "bta", 1)

        @block.vector
        def _(vector):
            vector.wait_ge(sems["bta"], 31)
            bisect_hi_update(vector)
            vector.tensor_copy(thk[:], i2b[:])
            vector.drain()
            vector.tensor_tensor(e3, key3, bc2(thk[:]), ALU.is_lt)
            vector.drain()
            vector.reduce_sum(rs2[:, 0:2], e3, AX.X)
            inc(vector.drain(), "btd", 1)

        @block.tensor
        def _(tensor):
            tensor.wait_ge(sems["btd"], 33)
            tensor.wait_ge(sems["bta"], 31)
            inc(tensor.matmul(pR, ONES, rs2[:, 0:2], start=True, stop=True), "btp", 1)

        @block.scalar
        def _(scalar):
            scalar.wait_ge(sems["btp"], 32)
            scalar.copy(t2b[:, 0:2], pR)
            inc(scalar.drain(), "bta", 1)

        @block.vector
        def _(vector):
            vector.wait_ge(sems["bta"], 32)
            vector.tensor_scalar(tt2[:], t2b[:, 0:2], -1.0, float(KH),
                                 ALU.mult, ALU.add)
            vector.tensor_tensor(e3, key3, bc2(thk[:]), ALU.is_lt)
            vector.tensor_tensor(v3, key3, bc2(thk[:]), ALU.is_equal)
            vector.drain()
            for b in range(Bsz):
                vector.tensor_tensor_scan(w128[:, 64 * b:64 * (b + 1)],
                                          ONES[:, 0:64], v128[:, 64 * b:64 * (b + 1)],
                                          0.0, ALU.mult, ALU.add)
            vector.drain()
            vector.tensor_copy(rs2[:, 0:1], w128[:, 63:64])
            vector.tensor_copy(rs2[:, 1:2], w128[:, 127:128])
            inc(vector.drain(), "eSel", 1)

        @block.tensor
        def _(tensor):
            tensor.wait_ge(sems["eSel"], 1)
            inc(tensor.matmul(pR, LSTR, rs2[:, 0:2], start=True, stop=True),
                "eSel", 1)

        @block.vector
        def _(vector):
            vector.wait_ge(sems["eSel"], 2)
            vector.tensor_copy(t2a[:, 0:2], pR)
            vector.drain()
            vector.tensor_sub(w128[:], w128[:], v128[:])
            vector.drain()
            vector.tensor_tensor(w3, w3, bc2(t2a[:, 0:2]), ALU.add)
            vector.drain()
            vector.tensor_tensor(w3, w3, bc2(tt2[:]), ALU.is_lt)
            vector.drain()
            vector.tensor_mul(v128[:], v128[:], w128[:])
            vector.drain()
            vector.tensor_add(e128[:], e128[:], v128[:])
            vector.drain()
            for b in range(Bsz):
                vector.tensor_tensor_scan(w128[:, 64 * b:64 * (b + 1)],
                                          ONES[:, 0:64], e128[:, 64 * b:64 * (b + 1)],
                                          0.0, ALU.mult, ALU.add)
            vector.drain()
            vector.tensor_copy(rs2[:, 0:1], w128[:, 63:64])
            vector.tensor_copy(rs2[:, 1:2], w128[:, 127:128])
            inc(vector.drain(), "eSel", 1)

        @block.tensor
        def _(tensor):
            tensor.wait_ge(sems["eSel"], 3)
            inc(tensor.matmul(pR, LSTR, rs2[:, 0:2], start=True, stop=True),
                "eSel", 1)

        @block.vector
        def _(vector):
            vector.wait_ge(sems["eSel"], 4)
            vector.tensor_copy(t2a[:, 0:2], pR)
            vector.drain()
            vector.tensor_sub(w128[:], w128[:], e128[:])
            vector.drain()
            vector.tensor_tensor(w3, w3, bc2(t2a[:, 0:2]), ALU.add)
            vector.drain()
            vector.tensor_scalar_add(w128[:, 64:128], w128[:, 64:128], float(KH))
            vector.drain()
            vector.tensor_scalar_add(w128[:], w128[:], -BIGRANK)
            vector.drain()
            vector.tensor_mul(w128[:], w128[:], e128[:])
            vector.drain()
            vector.tensor_scalar_add(w128[:], w128[:], BIGRANK)
            inc(vector.drain(), "eSel", 1)

        @block.sync
        def _(sync):
            sync.wait_ge(sems["eSel"], 5)
            for b in range(Bsz):
                inc(sync.dma_start(
                    out=rank_dram.ap()[b, :].rearrange("(p f) -> p f", p=128),
                    in_=w128[:, 64 * b:64 * (b + 1)]), "sMisc", 16)
        MISC_RANK = cnt["sMisc"]

        TR_L0 = cnt["eTrans"]
        for hch in range(4):
            for tt in range(4):
                g = hch * 4 + tt
                pl = pL0 if g % 2 == 0 else pL1

                @block.tensor
                def _(tensor, hch=hch, tt=tt, g=g, pl=pl):
                    tensor.wait_ge(sems["eLmul"], 13)
                    if g >= 2:
                        tensor.wait_ge(sems["eLmul"], 13 + g - 1)
                    for k in range(8):
                        inc(tensor.matmul(
                            pl[:, 0:512],
                            big4[:, 512 * k + 128 * hch:512 * k + 128 * (hch + 1)],
                            big0[:, 2048 * k + 512 * tt:2048 * k + 512 * (tt + 1)],
                            start=(k == 0), stop=(k == 7)), "eTrans", 1)

                @block.vector
                def _(vector, g=g, tt=tt, pl=pl):
                    vector.wait_ge(sems["eTrans"], TR_L0 + 8 * (g + 1))
                    if g >= 2:
                        vector.wait_ge(sems["eLH1"], g - 1)
                    vector.tensor_mul(scrC, pl[:, 0:512],
                                      rowb[:, 512 * tt:512 * (tt + 1)])
                    inc(vector.drain(), "eLmul", 1)

                @block.scalar
                def _(scalar, hch=hch, tt=tt, g=g):
                    scalar.wait_ge(sems["eLmul"], 13 + g + 1)
                    scalar.activation(
                        big2[:, 2048 * hch + 512 * tt:2048 * hch + 512 * (tt + 1)],
                        scrC, AF.Gelu,
                        bias=vec_sb[:, 16 + hch:17 + hch])
                    inc(scalar.drain(), "eLH1", 1)

        @block.scalar
        def _(scalar):
            scalar.wait_ge(sems["sMisc"], MISC_RANK)
            g0 = scalar.value_load(crow_sb[0:1, 2:3])
            rd_flat = rank_dram.ap().rearrange("b t -> (b t)")
            inc(scalar.dma_start(out=rk_loc.ap(),
                                 in_=bass.AP(tensor=rd_flat.tensor, offset=g0,
                                             ap=[[2048, 1], [1, 2048]])), "sRflR", 16)
            scalar.wait_ge(sems["sRflR"], 16)
            inc(scalar.dma_start(out=rkrow[0:1, :], in_=rk_loc.ap()), "sRkL", 16)
            for i in range(16):
                inc(scalar.dma_start(
                    out=rkc[:, i:i + 1],
                    in_=rk_loc.ap()[0, 128 * i:128 * (i + 1)]
                        .rearrange("(p o) -> p o", o=1)), "sRkL", 16)

        @block.vector
        def _(vector):
            vector.wait_ge(sems["sRkL"], 16 * 17)
            vector.tensor_reduce(pm1[0:1, 0:1], rkrow[0:1, :], AX.X, ALU.min)
            vector.drain()
            vector.tensor_scalar_min(pm1[0:1, 0:1], pm1[0:1, 0:1], float(NHG - 1))
            vector.drain()
            vector.tensor_copy(pm1i[0:1, 0:1], pm1[0:1, 0:1])
            vector.tensor_single_scalar(rkrow[0:1, :], rkrow[0:1, :],
                                        pm1[0:1, 0:1], ALU.subtract)
            vector.tensor_copy(rkci[:], rkc[:])
            inc(vector.drain(), "eSel", 1)

        @block.sync
        def _(sync):
            for i in range(16):
                inc(sync.dma_start(out=xsc.ap()[128 * i:128 * (i + 1), :],
                                   in_=zrow), "sZero", 16)
            for i in range(4):
                inc(sync.dma_start(out=h2full.ap()[NHG + 128 * i:NHG + 128 * (i + 1), :],
                                   in_=zrow[:, 0:Dm]), "sPadW", 16)
            for i in range(16):
                if i == 0:
                    sync.wait_ge(sems["eTrans"], 128)
                if i >= 1:
                    sync.wait_ge(sems["sXscW"], 32 * i)
                inc(sync.dma_start(out=xch[:],
                                   in_=x_tm[128 * i:128 * (i + 1), :]), "sXrl", 16)

        @block.gpsimd
        def _(gpsimd):
            gpsimd.wait_ge(sems["eSel"], 6)
            gpsimd.wait_ge(sems["sZero"], 256)
            for i in range(16):
                gpsimd.wait_ge(sems["sXrl"], 16 * (i + 1))
                inc(gpsimd.indirect_dma_start(
                    xsc.ap(), IndirectOffsetOnAxis(rkci[:, i:i + 1], 0),
                    xch[:], None,
                    bounds_check=NHG - 1, oob_is_err=False), "sXscW", 16)
                inc(gpsimd.indirect_dma_start(
                    xsc.ap(), IndirectOffsetOnAxis(rkci[:, i:i + 1], 0),
                    mv[:, 2 * i:2 * i + 2], None, element_offset=1024,
                    bounds_check=NHG - 1, oob_is_err=False), "sXscW", 16)
            gpsimd.wait_ge(sems["sXscW"], 16 * 32)
            for q in range(4):
                inc(gpsimd.collective_compute(
                    "AllReduce", ALU.add, replica_groups=[list(range(NCORE))],
                    ins=[xsc.ap()[512 * q:512 * (q + 1), :].opt()],
                    outs=[xsc_sum.ap()[512 * q:512 * (q + 1), :].opt()]), "ccAR", 1)

        @block.scalar
        def _(scalar):
            e0k = scalar.value_load(crow_sb[0:1, 1:2])
            scalar.wait_ge(sems["ccDU"], 4)
            for r_ in range(8):
                inc(scalar.dma_start(
                    out=d_loc.ap()[128 * r_:128 * (r_ + 1), :],
                    in_=bass.AP(tensor=agd_out.ap().tensor,
                                offset=e0k + 1024 * 1024 * r_,
                                ap=[[2048, 128], [1, 2048]])),
                    "sRfl0", 16)
                inc(scalar.dma_start(
                    out=u_loc.ap()[128 * r_:128 * (r_ + 1), :],
                    in_=bass.AP(tensor=agu_out.ap().tensor,
                                offset=e0k + 1024 * 1024 * r_,
                                ap=[[2048, 128], [1, 2048]])),
                    "sRfl0", 16)

            def emit_load(sup):
                r, h = sup // 2, sup % 2
                if sup >= 2:
                    scalar.wait_ge(sems["eSup"], sup - 1)
                scalar.wait_ge(sems["sRfl0"], 32 * (r + 1))
                par = sup % 2
                sl_ = "sScanL0" if par == 0 else "sScanL1"
                inc(scalar.dma_start(
                    out=DLT[par],
                    in_=d_loc.ap()[128 * r:128 * (r + 1),
                                   1024 * h:1024 * (h + 1)]), sl_, 16)
                inc(scalar.dma_start(
                    out=USB[par],
                    in_=u_loc.ap()[128 * r:128 * (r + 1),
                                   1024 * h:1024 * (h + 1)]), sl_, 16)

            emit_load(0)
            emit_load(1)
            for sup in range(16):
                for s in range(Sst):
                    idx = sup * 16 + s
                    scalar.wait_ge(sems["sScanL0" if sup % 2 == 0 else "sScanL1"],
                                   32 * (sup // 2 + 1))
                    if idx >= 2:
                        scalar.wait_ge(sems["eScan"], idx - 1)
                    inc(scalar.activation(ABR[idx % 2], DLT[sup % 2], AF.Exp,
                                          scale=napos[:, s:s + 1]), "eExp", 1)
                    if s == 0 and sup >= 1:
                        scalar.wait_ge(sems["eSup"], sup)
                        su = sup - 1
                        inc(scalar.dma_start(
                            out=agy_in.ap()[128 * (su // 4):128 * (su // 4) + 128,
                                            1024 * (su % 4):1024 * (su % 4) + 1024],
                            in_=YSB[1]), "sYW%d" % (su // 4), 16)
                if sup + 2 < 16:
                    emit_load(sup + 2)
            scalar.wait_ge(sems["eSup"], 16)
            inc(scalar.dma_start(out=agy_in.ap()[384:512, 3072:4096],
                                 in_=YSB[1]), "sYW3", 16)

        for sup in range(16):

            @block.vector
            def _(vector, sup=sup):
                par = sup % 2
                for s in range(Sst):
                    idx = sup * 16 + s
                    vector.wait_ge(sems["eExp"], idx + 1)
                    if s == 0 and sup % 8 == 0:
                        vector.memset(state[:], 0.0)
                        vector.drain()
                    if s == 1 and sup >= 1:
                        vector.wait_ge(sems["sYW%d" % ((sup - 1) // 4)],
                                       16 * ((sup - 1) % 4 + 1))
                    inc(vector.tensor_tensor_scan(
                        GSB[idx % 2], ABR[idx % 2], USB[par],
                        state[:, s:s + 1], ALU.mult, ALU.add), "eScan", 1)
                    if s == 0:
                        vector.tensor_scalar_mul(YSB[0], GSB[idx % 2],
                                                 apb_sb[:, 16 + s:17 + s])
                    else:
                        vector.scalar_tensor_tensor(
                            YSB[s % 2], GSB[idx % 2], apb_sb[:, 16 + s:17 + s],
                            YSB[(s + 1) % 2], ALU.mult, ALU.add)
                    vector.tensor_copy(state[:, s:s + 1], GSB[idx % 2][:, 1023:1024])
                vector.tensor_copy(tcol[:, 3:4], state[:, 0:1])
                inc(vector.drain(), "eSup", 1)

        @block.gpsimd
        def _(gpsimd):
            for q in range(4):
                gpsimd.wait_ge(sems["sYW%d" % q], 16 * 4)
                inc(gpsimd.collective_compute(
                    "AllGather", ALU.bypass, replica_groups=[list(range(NCORE))],
                    ins=[agy_in.ap()[128 * q:128 * (q + 1), :].opt()],
                    outs=[agy_all.ap()[1024 * q:1024 * (q + 1), :].opt()]), "ccY", 1)

        for k in range(2):

            @block.scalar
            def _(scalar, k=k):
                if k == 0:
                    scalar.wait_ge(sems["ccAR"], 4)
                else:
                    scalar.wait_ge(sems["eXgT"], 8 * k)
                s0x = scalar.value_load(crow_sb[0:1, 3:4])
                inc(scalar.dma_start(
                    out=XG_H,
                    in_=bass.AP(tensor=xsc_sum.ap().tensor,
                                offset=s0x + 128 * XCP * k,
                                ap=[[XCP, 128], [1, XCP]])), "sXgL", 16)

            @block.scalar
            def _(scalar, k=k):
                scalar.wait_ge(sems["sXgL"], 16 * (k + 1))
                vcol = XG_H[:, 1025:1026]
                scalar.activation(tcol[:, 0:1], vcol, AF.Ln, bias=EPS_LN)
                scalar.drain()
                scalar.activation(rhoc[:, k:k + 1], tcol[:, 0:1], AF.Exp, scale=-0.5)
                scalar.activation(rhoc[:, 4 + k:5 + k], tcol[:, 0:1], AF.Exp,
                                  scale=-1.0)
                inc(scalar.drain(), "eXgP", 1)

            @block.vector
            def _(vector, k=k):
                vector.wait_ge(sems["eXgP"], 12 * k + 1)
                vcol = XG_H[:, 1025:1026]
                vector.tensor_mul(tcol[:, 0:1], vcol, rhoc[:, 4 + k:5 + k])
                vector.drain()
                vector.tensor_scalar_mul(tcol[:, 0:1], tcol[:, 0:1], float(Dm))
                vector.drain()
                vector.tensor_scalar_max(rhoc[:, 8 + k:9 + k], tcol[:, 0:1],
                                         1e-24)
                inc(vector.drain(), "eXgP", 1)

            @block.scalar
            def _(scalar, k=k):
                scalar.wait_ge(sems["eXgP"], 12 * k + 2)
                scalar.activation(tcol[:, 0:1], rhoc[:, 8 + k:9 + k], AF.Ln)
                scalar.drain()
                scalar.activation(rhoc[:, 12 + k:13 + k], tcol[:, 0:1], AF.Exp,
                                  scale=-0.5)
                inc(scalar.drain(), "eXgP", 1)

            @block.vector
            def _(vector, k=k):
                vector.wait_ge(sems["eXgP"], 12 * k + 3)
                xk = XG_H[:, 0:1024]
                vector.tensor_mul(tcol[:, 0:1], rhoc[:, k:k + 1],
                                  rhoc[:, 12 + k:13 + k])
                vector.drain()
                vector.tensor_scalar_mul(tcol[:, 0:1], tcol[:, 0:1], SQD)
                vector.drain()
                vector.tensor_scalar(xk, xk, XG_H[:, 1024:1025],
                                     tcol[:, 0:1], ALU.subtract, ALU.mult)
                inc(vector.drain(), "eXgP", 1)

            @block.tensor
            def _(tensor, k=k):
                tensor.wait_ge(sems["eXgP"], 12 * k + 4)
                for j in range(8):
                    idx = k * 8 + j
                    if idx >= 2:
                        tensor.wait_ge(sems["eXgT"], idx - 1)
                    inc(tensor.transpose(
                        pT[idx % 2], XG_H[:, 128 * j:128 * (j + 1)], IDN),
                        "eXgP", 1)

            @block.scalar
            def _(scalar, k=k):
                for j in range(8):
                    idx = k * 8 + j
                    scalar.wait_ge(sems["eXgP"], 12 * k + 4 + j + 1)
                    scalar.activation(
                        XGN[:, 256 * j + 128 * k:256 * j + 128 * (k + 1)],
                        pT[idx % 2], AF.Copy, scale=vec_sb[:, 68 + j:69 + j])
                    inc(scalar.drain(), "eXgT", 1)

        @block.sync
        def _(sync):
            sync.wait_ge(sems["ccAR"], 4)
            for m in range(32):
                if m >= 1:
                    sync.wait_ge(sems["eH1m"], 8 * m)
                inc(sync.dma_start(
                    out=big5[:, 0:1024].rearrange("p (k d) -> p k d", k=8),
                    in_=w1hb[1024 * m:1024 * (m + 1), :]
                        .rearrange("(k p) d -> p k d", p=128)), "sW1L", 16)

        for m in range(32):
            pl = pD0 if m % 2 == 0 else pD1

            @block.tensor
            def _(tensor, m=m, pl=pl):
                tensor.wait_ge(sems["sW1L"], 16 * (m + 1))
                tensor.wait_ge(sems["eXgT"], 16)
                if m >= 2:
                    tensor.wait_ge(sems["eGelH"], m - 1)
                for k in range(8):
                    inc(tensor.matmul(
                        pl[:, 0:256],
                        big5[:, 128 * k:128 * (k + 1)],
                        XGN[:, 256 * k:256 * (k + 1)],
                        start=(k == 0), stop=(k == 7)), "eH1m", 1)

            @block.scalar
            def _(scalar, m=m, pl=pl):
                scalar.wait_ge(sems["eH1m"], 8 * (m + 1))
                scalar.activation(big1[:, 256 * m:256 * (m + 1)], pl[:, 0:256],
                                  AF.Gelu, bias=vec_sb[:, 28 + m:29 + m])
                inc(scalar.drain(), "eGelH", 1)
        H1M_END = cnt["eH1m"]

        @block.sync
        def _(sync):
            for m in range(8):
                for hf in range(2):
                    if m + hf >= 1:
                        sync.wait_ge(sems["eH2m"], 32 * m + 16 * hf)
                    inc(sync.dma_start(
                        out=big5[:, 1024:3072].rearrange("p (k d) -> p k d", k=16),
                        in_=w2hb[4096 * m + 2048 * hf:4096 * m + 2048 * (hf + 1), :]
                            .rearrange("(k p) d -> p k d", p=128)), "sW2L", 16)

        for m in range(8):
            pl = pL0 if m % 2 == 0 else pL1

            @block.tensor
            def _(tensor, m=m, pl=pl):
                tensor.wait_ge(sems["eGelH"], 32)
                if m >= 2:
                    tensor.wait_ge(sems["eH2e"], m - 1)
                for hf in range(2):
                    tensor.wait_ge(sems["sW2L"], 16 * (2 * m + hf + 1))
                    for kk in range(16):
                        k = 16 * hf + kk
                        inc(tensor.matmul(
                            pl[:, 0:256],
                            big5[:, 1024 + 128 * kk:1024 + 128 * (kk + 1)],
                            big1[:, 256 * k:256 * (k + 1)],
                            start=(k == 0), stop=(k == 31)), "eH2m", 1)

            @block.vector
            def _(vector, m=m, pl=pl):
                vector.wait_ge(sems["eH2m"], 32 * (m + 1))
                vector.tensor_scalar_add(scrH[:, 256 * (m % 2):256 * (m % 2) + 256],
                                         pl[:, 0:256],
                                         vec_sb[:, 60 + m:61 + m])
                inc(vector.drain(), "eH2e", 1)

            @block.tensor
            def _(tensor, m=m):
                tensor.wait_ge(sems["eH2e"], m + 1)
                for kk in range(2):
                    idx = 2 * m + kk
                    if idx >= 2:
                        tensor.wait_ge(sems["eH2T"], idx - 1)
                    inc(tensor.transpose(
                        pT[idx % 2],
                        scrH[:, 256 * (m % 2) + 128 * kk:256 * (m % 2) + 128 * (kk + 1)],
                        IDN), "eH1m", 1)

            @block.vector
            def _(vector, m=m):
                for kk in range(2):
                    idx = 2 * m + kk
                    vector.wait_ge(sems["eH1m"], H1M_END + idx + 1)
                    vector.tensor_copy(
                        h2tm[:, 1024 * kk + 128 * m:1024 * kk + 128 * (m + 1)],
                        pT[idx % 2])
                    inc(vector.drain(), "eH2T", 1)

        @block.sync
        def _(sync):
            sync.wait_ge(sems["eH2T"], 16)
            inc(sync.dma_start(
                out=ag_h2_in.ap().rearrange("(kk p) d -> p kk d", p=128),
                in_=h2tm.rearrange("p (kk d) -> p kk d", kk=2)), "sH2W", 16)

        @block.gpsimd
        def _(gpsimd):
            gpsimd.wait_ge(sems["sH2W"], 16)
            gpsimd.wait_ge(sems["sPadW"], 64)
            inc(gpsimd.collective_compute(
                "AllGather", ALU.bypass, replica_groups=[list(range(NCORE))],
                ins=[ag_h2_in.ap().opt()], outs=[h2full.ap()[0:NHG, :].opt()]),
                "ccH2", 1)

        @block.scalar
        def _(scalar):
            scalar.wait_ge(sems["ccH2"], 1)
            pv = scalar.value_load(pm1i[0:1, 0:1])
            inc(scalar.dma_start(
                out=H2OWN[:].rearrange("p (k f) -> p k f", k=4),
                in_=bass.AP(tensor=h2full.ap().tensor, offset=pv * Dm,
                            ap=[[Dm, 128], [128 * Dm, 4], [1, Dm]])), "sOwnL", 16)

        @block.tensor
        def _(tensor):
            tensor.wait_ge(sems["eSel"], 6)
            tensor.wait_ge(sems["eLmul"], 29)
            for q in range(4):
                pf = pF0 if q % 2 == 0 else pF1
                if q >= 2:
                    tensor.wait_ge(sems["eRelB"], q - 1)
                inc(tensor.matmul(pf[:, 0:512], ONE_ROW,
                                  rkrow[0:1, 512 * q:512 * (q + 1)],
                                  start=True, stop=True), "eYb", 1)

        @block.vector
        def _(vector):
            for q in range(4):
                pf = pF0 if q % 2 == 0 else pF1
                vector.wait_ge(sems["eYb"], 4 + q + 1)
                vector.tensor_copy(rowb[:, 512 * q:512 * (q + 1)], pf[:, 0:512])
                inc(vector.drain(), "eRelB", 1)

        @block.sync
        def _(sync):
            sync.wait_ge(sems["eH2m"], 8 * 32)
            for k in range(8):
                inc(sync.dma_start(out=big1[:, 1024 * k:1024 * (k + 1)],
                                   in_=woT[128 * k:128 * (k + 1), :]), "sFinW", 16)
            sync.wait_ge(sems["eLH1"], 16)
            for k in range(4):
                inc(sync.dma_start(out=big4[:, 1024 * k:1024 * (k + 1)],
                                   in_=w2lT[128 * k:128 * (k + 1), :]), "sFinW", 16)



        @block.sync
        def _(sync):
            sync.wait_ge(sems["eH1m"], 256)
            yoff = [sync.value_load(crow_sb[0:1, 4 + t:5 + t]) for t in range(4)]
            sync.wait_ge(sems["ccY"], 4)
            sync.wait_ge(sems["sH2W"], 16)
            for tt in range(4):
                if tt >= 1:
                    sync.wait_ge(sems["eFinM"], 16 * 8 * tt)
                inc(sync.dma_start(
                    out=ybig[:].rearrange("p (k f) -> p k f", k=8),
                    in_=bass.AP(tensor=agy_all.ap().tensor, offset=yoff[tt],
                                ap=[[4096, 128], [128 * 4096, 8], [1, 512]])),
                    "sFinY", 16)
                for dch in range(8):
                    g = tt * 8 + dch
                    inc(sync.dma_start(
                        out=xfin[:, 512 * (g % 2):512 * (g % 2 + 1)],
                        in_=x_fm[128 * dch:128 * (dch + 1),
                                 512 * tt:512 * (tt + 1)]), "sFinX", 16)
                    sync.wait_ge(sems["eFin"], g + 1)
                    inc(sync.dma_start(
                        out=out_ext[128 * dch:128 * (dch + 1),
                                    512 * tt:512 * (tt + 1)],
                        in_=outt[:, 512 * (g % 2):512 * (g % 2 + 1)]), "sOut", 16)

        for tt in range(4):

            @block.vector
            def _(vector, tt=tt):
                vector.wait_ge(sems["eRelB"], 4)
                if tt >= 1:
                    vector.wait_ge(sems["eFinM"], 16 * 8 * tt)
                for sc in range(4):
                    vector.tensor_scalar(Stt[:, 512 * sc:512 * (sc + 1)],
                                         rowb[:, 512 * tt:512 * (tt + 1)],
                                         IOTC[:, sc:sc + 1], None, ALU.is_equal)
                    inc(vector.drain(), "eStt", 1)

            for dch in range(8):
                g = tt * 8 + dch
                pf = pF0 if g % 2 == 0 else pF1

                @block.tensor
                def _(tensor, tt=tt, dch=dch, g=g, pf=pf):
                    tensor.wait_ge(sems["sFinW"], 16 * 12)
                    tensor.wait_ge(sems["sFinY"], 16 * (tt + 1))
                    tensor.wait_ge(sems["eStt"], 4 * (tt + 1))
                    tensor.wait_ge(sems["sOwnL"], 16)
                    if g == 0:
                        tensor.wait_ge(sems["eRelB"], 4)
                    if g >= 2:
                        tensor.wait_ge(sems["eFin"], g - 1)
                    for k in range(8):
                        inc(tensor.matmul(
                            pf[:, 0:512],
                            big1[:, 1024 * k + 128 * dch:1024 * k + 128 * (dch + 1)],
                            ybig[:, 512 * k:512 * (k + 1)],
                            start=(k == 0), stop=False), "eFinM", 1)
                    for k in range(4):
                        inc(tensor.matmul(
                            pf[:, 0:512],
                            big4[:, 1024 * k + 128 * dch:1024 * k + 128 * (dch + 1)],
                            big2[:, 2048 * k + 512 * tt:2048 * k + 512 * (tt + 1)],
                            start=False, stop=False), "eFinM", 1)
                    for k in range(4):
                        inc(tensor.matmul(
                            pf[:, 0:512],
                            H2OWN[:, 1024 * k + 128 * dch:1024 * k + 128 * (dch + 1)],
                            Stt[:, 512 * k:512 * (k + 1)],
                            start=False, stop=(k == 3)), "eFinM", 1)

                @block.vector
                def _(vector, tt=tt, dch=dch, g=g, pf=pf):
                    vector.wait_ge(sems["eFinM"], 16 * (g + 1))
                    vector.wait_ge(sems["sFinX"], 16 * (g + 1))
                    vector.scalar_tensor_tensor(
                        outt[:, 512 * (g % 2):512 * (g % 2 + 1)], pf[:, 0:512],
                        bstar[:, dch:dch + 1],
                        xfin[:, 512 * (g % 2):512 * (g % 2 + 1)],
                        ALU.add, ALU.add)
                    inc(vector.drain(), "eFin", 1)

        @block.sync
        def _(sync):
            sync.wait_ge(sems["sOut"], 16 * 32)

    _es.close()
    for cm in reversed(ctx):
        cm.__exit__(None, None, None)
    return nc


_NC_CACHE = None


def _get_nc():
    global _NC_CACHE
    if _NC_CACHE is None:
        _NC_CACHE = build_nc()
    return _NC_CACHE


def kernel(L0_attn_x, L0_attn_Wd, L0_attn_bd, L0_attn_a_log, L0_attn_b_param,
           L0_attn_Wo, L0_attn_bo, L0_ffn_routing_token, L0_ffn_gamma_l,
           L0_ffn_w1_l, L0_ffn_b1_l, L0_ffn_w2_l, L0_ffn_b2_l, L0_ffn_gamma_h,
           L0_ffn_w1_h, L0_ffn_b1_h, L0_ffn_w2_h, L0_ffn_b2_h):
    f = np.asarray
    x = np.ascontiguousarray(f(L0_attn_x, dtype=np.float32)).reshape(NT, Dm)

    def colsplit(v, nch):
        v = np.ascontiguousarray(f(v, dtype=np.float32)).reshape(nch, 128)
        return v.T

    wdT = np.ascontiguousarray(f(L0_attn_Wd, dtype=np.float32).T)
    woT = np.ascontiguousarray(f(L0_attn_Wo, dtype=np.float32).T)
    w1lT = np.ascontiguousarray(f(L0_ffn_w1_l, dtype=np.float32).T)
    w2lT = np.ascontiguousarray(f(L0_ffn_w2_l, dtype=np.float32).T)
    w1hT = f(L0_ffn_w1_h, dtype=np.float32).T
    w2hT = f(L0_ffn_w2_h, dtype=np.float32).T
    w1hb = np.ascontiguousarray(
        w1hT.reshape(Dm, 32, 128).transpose(1, 0, 2)).reshape(32 * Dm, 128)
    w2hb = np.ascontiguousarray(
        w2hT.reshape(4096, 8, 128).transpose(1, 0, 2)).reshape(8 * 4096, 128)

    vecs = np.zeros((128, 92), np.float32)
    vecs[:, 0:8] = colsplit(L0_attn_bd, 8)
    vecs[:, 8:16] = colsplit(L0_attn_bo, 8)
    vecs[:, 16:20] = colsplit(L0_ffn_b1_l, 4)
    vecs[:, 20:28] = colsplit(L0_ffn_b2_l, 8)
    vecs[:, 28:60] = colsplit(L0_ffn_b1_h, 32)
    vecs[:, 60:68] = colsplit(L0_ffn_b2_h, 8)
    vecs[:, 68:76] = colsplit(L0_ffn_gamma_h, 8)
    vecs[:, 76:84] = colsplit(L0_ffn_routing_token, 8)
    vecs[:, 84:92] = colsplit(L0_ffn_gamma_l, 8)

    ones = np.ones((128, 128), np.float32)
    lstr = (np.arange(128)[:, None] < np.arange(128)[None, :]).astype(np.float32)
    idn = np.eye(128, dtype=np.float32)
    iotc = (np.arange(128)[:, None] + 128 * np.arange(4)[None, :]).astype(np.float32)
    fco = np.zeros((128, 2), np.float32)
    fco[:, 0] = EPS_LN
    fco[:, 1] = -1e-4
    consts = np.concatenate([ones, lstr, idn, iotc, fco], axis=1)

    a_log = f(L0_attn_a_log, dtype=np.float32)
    b_par = f(L0_attn_b_param, dtype=np.float32)

    in_maps = []
    for c in range(NCORE):
        g0 = TOK * c
        apbc = np.concatenate([a_log[128 * c:128 * (c + 1)],
                               b_par[128 * c:128 * (c + 1)]], axis=1)
        crow_v = np.zeros((1, 8), np.int32)
        crow_v[0, 0] = c
        crow_v[0, 1] = 8388608 * (c // 4) + 262144 * (c % 4)
        crow_v[0, 2] = g0
        crow_v[0, 3] = SLOT * c * XCP
        for t in range(4):
            gt = g0 + 512 * t
            q = gt // 4096
            col = gt % 4096
            crow_v[0, 4 + t] = (1024 * q) * 4096 + col
        in_maps.append(dict(
            x_tm=np.ascontiguousarray(x[g0:g0 + TOK]),
            x_fm=np.ascontiguousarray(x[g0:g0 + TOK].T),
            wdT=wdT, woT=woT, w1lT=w1lT, w2lT=w2lT, w1hb=w1hb, w2hb=w2hb,
            vecs=vecs, apb=np.ascontiguousarray(apbc), consts=consts, crow=crow_v,
        ))

    nc = _get_nc()
    res = run_bass_kernel_spmd(nc, in_maps, core_ids=list(range(NCORE)))
    out = np.empty((Bsz, Tseq, Dm), np.float32)
    for c in range(NCORE):
        o = res.results[c]["out"]
        b, t0 = c // 4, (c % 4) * TOK
        out[b, t0:t0 + TOK] = o.T
    return out


# version 11
# speedup vs baseline: 1.0800x; 1.0691x over previous
import numpy as np
import concourse.bass as bass
import concourse.mybir as mybir
from concourse.bass_utils import run_bass_kernel_spmd
from concourse.bass_core_rust import IndirectOffsetOnAxis

F32 = mybir.dt.float32
I32 = mybir.dt.int32
AF = mybir.ActivationFunctionType
ALU = mybir.AluOpType
AX = mybir.AxisListType

NCORE = 8
Bsz = 2
Tseq = 8192
Dm = 1024
Sst = 16
TOK = 2048
NT = Bsz * Tseq
KH = 1024
NHG = 2048
KEFF = 1152.0
NIT = 50
SLOT = 256
WIN = 512
XCP = 1028
EPS_LN = 1e-5
BIGRANK = 9999.0
SQD = float(np.sqrt(Dm))

A1 = float(np.float32(np.float32(np.log(np.float32(KEFF)))
                      - np.float32(np.log(np.float32(8192.0)))))
NLOGK = float(-np.float32(np.log(np.float32(KEFF))))


def build_nc():
    nc = bass.Bass("TRN2", target_bir_lowering=False, debug=False,
                   detect_race_conditions=False)

    x_tm = nc.dram_tensor("x_tm", [TOK, Dm], F32, kind="ExternalInput").ap()
    x_fm = nc.dram_tensor("x_fm", [Dm, TOK], F32, kind="ExternalInput").ap()
    wdT = nc.dram_tensor("wdT", [Dm, Dm], F32, kind="ExternalInput").ap()
    woT = nc.dram_tensor("woT", [Dm, Dm], F32, kind="ExternalInput").ap()
    w1lT = nc.dram_tensor("w1lT", [Dm, 512], F32, kind="ExternalInput").ap()
    w2lT = nc.dram_tensor("w2lT", [512, Dm], F32, kind="ExternalInput").ap()
    w1hb = nc.dram_tensor("w1hb", [32 * Dm, 128], F32, kind="ExternalInput").ap()
    w2hb = nc.dram_tensor("w2hb", [8 * 4096, 128], F32, kind="ExternalInput").ap()
    vecs = nc.dram_tensor("vecs", [128, 92], F32, kind="ExternalInput").ap()
    apb = nc.dram_tensor("apb", [128, 32], F32, kind="ExternalInput").ap()
    consts = nc.dram_tensor("consts", [128, 390], F32, kind="ExternalInput").ap()
    crow = nc.dram_tensor("crow", [1, 8], I32, kind="ExternalInput").ap()

    out_ext = nc.dram_tensor("out", [Dm, TOK], F32, kind="ExternalOutput").ap()

    ag_s_in = nc.dram_tensor("ag_s_in", [1, TOK], F32)
    ag_s_out = nc.dram_tensor("ag_s_out", [8, TOK], F32)
    agd_in = nc.dram_tensor("agd_in", [Dm, TOK], F32)
    agu_in = nc.dram_tensor("agu_in", [Dm, TOK], F32)
    agd_out = nc.dram_tensor("agd_out", [2 * 8 * 512, TOK], F32)
    agu_out = nc.dram_tensor("agu_out", [2 * 8 * 512, TOK], F32)
    agy_in = nc.dram_tensor("agy_in", [512, 4096], F32)
    agy_all = nc.dram_tensor("agy_all", [4 * 1024, 4096], F32)
    rank_dram = nc.dram_tensor("rank_dram", [Bsz, Tseq], F32)
    rk_loc = nc.dram_tensor("rk_loc", [1, TOK], F32)
    d_loc = nc.dram_tensor("d_loc", [8 * 128, TOK], F32)
    u_loc = nc.dram_tensor("u_loc", [8 * 128, TOK], F32)
    xsc = nc.dram_tensor("xsc", [NHG, XCP], F32)
    xsc_sum = nc.dram_tensor("xsc_sum", [NHG, XCP], F32)
    ag_h2_in = nc.dram_tensor("ag_h2_in", [SLOT, Dm], F32)
    h2full = nc.dram_tensor("h2full", [NHG + WIN, Dm], F32)

    sems = {}
    cnt = {}
    ctx = []

    def sem(name):
        cm = nc.semaphore(name)
        sems[name] = cm.__enter__()
        ctx.append(cm)
        cnt[name] = 0

    for n in ["sLdA", "sLdX", "sDW0", "sDW1", "sSw", "sScanL0", "sScanL1",
              "sYW0", "sYW1", "sYW2", "sYW3", "sMisc", "sZero", "sXrl",
              "sXscW", "sRkL", "sRflR", "sRfl0", "sRfl1",
              "sXgL", "sW1L", "sW2L", "sOwnL", "sFinW", "sFinY", "sFinX",
              "sOut", "sRhoL", "sH2W", "sPadW", "sRfl",
              "eStat", "eVar", "eXn", "eTrans", "eEvacXn", "eDelta", "eU",
              "eRho", "eLH1", "eLmul", "eSrow", "eSq", "rtd", "rta", "rtp",
              "btd", "bta", "btp", "eSel", "eRelB", "eExp", "eScan", "eSup",
              "eXgP", "eXgT", "eH1m", "eGelH", "eH2m", "eH2e", "eH2T",
              "eFinM", "eFin", "eYb", "eStt",
              "ccS", "ccDU", "ccY", "ccAR", "ccH2"]:
        sem(n)

    def inc(instr, name, amt):
        cnt[name] += amt
        instr.then_inc(sems[name], amt)
        return cnt[name]

    import contextlib
    _es = contextlib.ExitStack()
    big0 = _es.enter_context(nc.sbuf_tensor("big0", [128, 16384], F32))
    big1 = _es.enter_context(nc.sbuf_tensor("big1", [128, 8192], F32))
    big2 = _es.enter_context(nc.sbuf_tensor("big2", [128, 8192], F32))
    big4 = _es.enter_context(nc.sbuf_tensor("big4", [128, 4096], F32))
    big5 = _es.enter_context(nc.sbuf_tensor("big5", [128, 3072], F32))
    big6 = _es.enter_context(nc.sbuf_tensor("big6", [128, 3076], F32))
    ybig = _es.enter_context(nc.sbuf_tensor("ybig", [128, 4096], F32))
    xch = _es.enter_context(nc.sbuf_tensor("xch", [128, 1024], F32))
    rowb = _es.enter_context(nc.sbuf_tensor("rowb", [128, 2048], F32))
    Stt = _es.enter_context(nc.sbuf_tensor("Stt", [128, 2048], F32))
    mv = _es.enter_context(nc.sbuf_tensor("mv", [128, 32], F32))
    rhoc = _es.enter_context(nc.sbuf_tensor("rhoc", [128, 16], F32))
    vec_sb = _es.enter_context(nc.sbuf_tensor("vec_sb", [128, 92], F32))
    bstar = _es.enter_context(nc.sbuf_tensor("bstar", [128, 8], F32))
    apb_sb = _es.enter_context(nc.sbuf_tensor("apb_sb", [128, 32], F32))
    napos = _es.enter_context(nc.sbuf_tensor("napos", [128, 16], F32))
    cst = _es.enter_context(nc.sbuf_tensor("cst", [128, 390], F32))
    crow_sb = _es.enter_context(nc.sbuf_tensor("crow_sb", [1, 8], I32))
    s128 = _es.enter_context(nc.sbuf_tensor("s128", [128, 128], F32))
    w128 = _es.enter_context(nc.sbuf_tensor("w128", [128, 128], F32))
    m2 = _es.enter_context(nc.sbuf_tensor("m2", [128, 2], F32))
    na2 = _es.enter_context(nc.sbuf_tensor("na2", [128, 2], F32))
    mm2 = _es.enter_context(nc.sbuf_tensor("mm2", [128, 2], F32))
    t2a = _es.enter_context(nc.sbuf_tensor("t2a", [128, 2], F32))
    t2b = _es.enter_context(nc.sbuf_tensor("t2b", [128, 2], F32))
    rs2 = _es.enter_context(nc.sbuf_tensor("rs2", [128, 2], F32))
    tcol = _es.enter_context(nc.sbuf_tensor("tcol", [128, 4], F32))
    i2a = _es.enter_context(nc.sbuf_tensor("i2a", [128, 2], I32))
    i2b = _es.enter_context(nc.sbuf_tensor("i2b", [128, 2], I32))
    i2c = _es.enter_context(nc.sbuf_tensor("i2c", [128, 2], I32))
    i2d = _es.enter_context(nc.sbuf_tensor("i2d", [128, 2], I32))
    i2e = _es.enter_context(nc.sbuf_tensor("i2e", [128, 2], I32))
    thk = _es.enter_context(nc.sbuf_tensor("thk", [128, 2], I32))
    tt2 = _es.enter_context(nc.sbuf_tensor("tt2", [128, 2], F32))
    sc2 = _es.enter_context(nc.sbuf_tensor("sc2", [128, 2], F32))
    pD0 = _es.enter_context(nc.psum_tensor("pD0", [128, 512], F32))
    pD1 = _es.enter_context(nc.psum_tensor("pD1", [128, 512], F32))
    pL0 = _es.enter_context(nc.psum_tensor("pL0", [128, 512], F32))
    pL1 = _es.enter_context(nc.psum_tensor("pL1", [128, 512], F32))
    pF0 = _es.enter_context(nc.psum_tensor("pF0", [128, 512], F32))
    pF1 = _es.enter_context(nc.psum_tensor("pF1", [128, 512], F32))
    pTA = _es.enter_context(nc.psum_tensor("pTA", [128, 512], F32))
    pTB = _es.enter_context(nc.psum_tensor("pTB", [128, 512], F32))
    block = _es.enter_context(nc.Block())
    zrow = big6[:, 0:1028]
    XG_H = big6[:, 0:1028]
    XGN = big6[:, 1028:3076]
    scrC = big6[:, 1028:1540]
    scrH = big6[:, 1028:1540]
    xfin = big6[:, 0:1024]
    outt = big6[:, 1028:2052]
    dlu = ybig[:, 0:2048]
    h2tm = ybig[:, 2048:4096]
    rolrow = ybig[0:1, 2048:4096]
    scrB = Stt[:, 0:1024]
    sev = Stt[0:1, 1024:1536]
    rkrow = Stt[0:1, 0:2048]
    scrR = rowb[:, 0:80]
    t16 = w128[0:16, 0:128]
    e128 = Stt[:, 1024:1152]
    v128 = Stt[:, 1152:1280]
    tr2 = Stt[0:2, 1280:1408]
    ms1 = Stt[0:1, 1408:1410]
    pm1 = tcol[0:1, 1:2]
    pm1i = tcol[0:1, 2:3].bitcast(I32)
    rkc = ybig[:, 4:20]
    rkci = ybig[:, 20:36].bitcast(I32)
    state = ybig[:, 36:52]
    if True:
        pT = [pTA[:, 0:128], pTB[:, 0:128]]
        pR = pTA[:, 384:386]
        pS = pTB[0:1, 256:768] if False else pTB[0:1, 0:512]

        ONES = cst[:, 0:128]
        ONE_ROW = cst[0:1, 0:128]
        LSTR = cst[:, 128:256]
        IDN = cst[:, 256:384]
        IOTC = cst[:, 384:388]
        nc.const_aps.register(F32, EPS_LN, cst[:, 388:389])
        nc.const_aps.register(F32, -1e-4, cst[:, 389:390])
        DLT = [big0[:, 0:1024], big0[:, 1024:2048]]
        USB = [big0[:, 2048:3072], big0[:, 3072:4096]]
        ABR = [big0[:, 4096:5120], big0[:, 5120:6144]]
        GSB = [big0[:, 6144:7168], big0[:, 7168:8192]]
        YSB = [big0[:, 8192:9216], big0[:, 9216:10240]]
        H2OWN = big0[:, 10240:14336]
        s3 = s128[:].rearrange("p (b f) -> p b f", b=2)
        v3 = v128[:].rearrange("p (b f) -> p b f", b=2)
        e3 = e128[:].rearrange("p (b f) -> p b f", b=2)
        w3 = w128[:].rearrange("p (b f) -> p b f", b=2)
        key3 = w128[:].bitcast(I32).rearrange("p (b f) -> p b f", b=2)

        def bc2(ap2):
            return ap2.unsqueeze(2).to_broadcast([128, 2, 64])

        @block.sync
        def _(sync):
            inc(sync.dma_start(out=cst[:], in_=consts), "sLdA", 16)
            inc(sync.dma_start(out=vec_sb[:], in_=vecs), "sLdA", 16)
            inc(sync.dma_start(out=apb_sb[:], in_=apb), "sLdA", 16)
            inc(sync.dma_start(out=crow_sb[:], in_=crow), "sLdA", 16)
            for k in range(8):
                inc(sync.dma_start(out=big1[:, 1024 * k:1024 * (k + 1)],
                                   in_=wdT[128 * k:128 * (k + 1), :]), "sLdA", 16)
            for k in range(8):
                inc(sync.dma_start(out=big4[:, 512 * k:512 * (k + 1)],
                                   in_=w1lT[128 * k:128 * (k + 1), :]), "sLdA", 16)
            for ch in range(16):
                if ch >= 1:
                    sync.wait_ge(sems["eTrans"], 8 * ch)
                inc(sync.dma_start(out=xch[:],
                                   in_=x_tm[128 * ch:128 * (ch + 1), :]), "sLdX", 16)
        LDA_X = [16 * (ch + 1) for ch in range(16)]

        @block.scalar
        def _(scalar):
            scalar.wait_ge(sems["sLdA"], 16 * 20)
            scalar.activation(scrB[:, 0:16], apb_sb[:, 0:16], AF.Exp)
            scalar.drain()
            scalar.activation(scrB[:, 16:32], scrB[:, 0:16], AF.Ln, bias=1.0)
            scalar.drain()
            scalar.activation(napos[:], scrB[:, 16:32], AF.Copy,
                              bias=-1e-4, scale=-1.0)
            inc(scalar.drain(), "eRho", 1)

        @block.vector
        def _(vector):
            vector.wait_ge(sems["sLdA"], 16 * 20)
            vector.memset(zrow, 0.0)
            vector.memset(state[:], 0.0)
            inc(vector.tensor_add(bstar[:], vec_sb[:, 8:16], vec_sb[:, 20:28]),
                "eStat", 1)

        for ch in range(16):
            sl = xch[:]

            @block.vector
            def _(vector, ch=ch, sl=sl):
                vector.wait_ge(sems["sLdX"], LDA_X[ch])
                vector.reduce_sum(mv[:, 2 * ch:2 * ch + 1], sl, AX.X)
                inc(vector.drain(), "eStat", 1)

            @block.scalar
            def _(scalar, ch=ch, sl=sl):
                scalar.wait_ge(sems["sLdX"], LDA_X[ch])
                scalar.activation(big5[:, 0:1024], sl, AF.Square)
                inc(scalar.drain(), "eSq", 1)

            @block.vector
            def _(vector, ch=ch):
                vector.wait_ge(sems["eSq"], ch + 1)
                vector.reduce_sum(mv[:, 2 * ch + 1:2 * ch + 2],
                                  big5[:, 0:1024], AX.X)
                inc(vector.drain(), "eStat", 1)

            @block.vector
            def _(vector, ch=ch):
                vector.wait_ge(sems["eStat"], 1 + 2 * (ch + 1))
                m_ = mv[:, 2 * ch:2 * ch + 1]
                v_ = mv[:, 2 * ch + 1:2 * ch + 2]
                vector.tensor_scalar_mul(m_, m_, 1.0 / Dm)
                vector.tensor_scalar_mul(v_, v_, 1.0 / Dm)
                vector.drain()
                vector.tensor_mul(tcol[:, 0:1], m_, m_)
                vector.drain()
                vector.tensor_sub(v_, v_, tcol[:, 0:1])
                inc(vector.drain(), "eVar", 1)

            @block.scalar
            def _(scalar, ch=ch):
                scalar.wait_ge(sems["eVar"], ch + 1)
                scalar.activation(tcol[:, 1:2], mv[:, 2 * ch + 1:2 * ch + 2],
                                  AF.Ln, bias=EPS_LN)
                scalar.drain()
                scalar.activation(rhoc[:, ch:ch + 1], tcol[:, 1:2], AF.Exp,
                                  scale=-0.5)
                inc(scalar.drain(), "eRho", 1)

            @block.vector
            def _(vector, ch=ch, sl=sl):
                vector.wait_ge(sems["eRho"], 1 + ch + 1)
                vector.tensor_scalar(sl, sl, mv[:, 2 * ch:2 * ch + 1],
                                     rhoc[:, ch:ch + 1], ALU.subtract, ALU.mult)
                inc(vector.drain(), "eXn", 1)

            @block.tensor
            def _(tensor, ch=ch, sl=sl):
                tensor.wait_ge(sems["eXn"], ch + 1)
                for j in range(8):
                    idx = ch * 8 + j
                    if idx >= 2:
                        tensor.wait_ge(sems["eEvacXn"], idx - 1)
                    inc(tensor.transpose(pT[idx % 2], sl[:, 128 * j:128 * (j + 1)],
                                         IDN), "eTrans", 1)

            @block.scalar
            def _(scalar, ch=ch):
                for j in range(8):
                    idx = ch * 8 + j
                    scalar.wait_ge(sems["eTrans"], idx + 1)
                    scalar.copy(
                        big0[:, 2048 * j + 128 * ch:2048 * j + 128 * (ch + 1)],
                        pT[idx % 2])
                    inc(scalar.drain(), "eEvacXn", 1)

        @block.scalar
        def _(scalar):
            scalar.wait_ge(sems["eVar"], 16)
            vcols = mv[:].rearrange("p (c two) -> p two c", two=2)[:, 1, :]
            scalar.activation(scrR[:, 0:16], vcols, AF.Ln, bias=EPS_LN)
            scalar.drain()
            scalar.activation(scrR[:, 16:32], scrR[:, 0:16], AF.Exp, scale=-1.0)
            inc(scalar.drain(), "eRho", 1)

        @block.vector
        def _(vector):
            vector.wait_ge(sems["eRho"], 18)
            vcols = mv[:].rearrange("p (c two) -> p two c", two=2)[:, 1, :]
            vector.tensor_mul(scrR[:, 32:48], vcols, scrR[:, 16:32])
            vector.drain()
            vector.tensor_scalar_mul(scrR[:, 32:48], scrR[:, 32:48], float(Dm))
            vector.drain()
            vector.tensor_scalar_max(scrR[:, 32:48], scrR[:, 32:48], 1e-24)
            inc(vector.drain(), "eLmul", 1)

        @block.scalar
        def _(scalar):
            scalar.wait_ge(sems["eLmul"], 1)
            scalar.activation(scrR[:, 48:64], scrR[:, 32:48], AF.Ln)
            scalar.drain()
            scalar.activation(scrR[:, 64:80], scrR[:, 48:64], AF.Exp, scale=-0.5)
            inc(scalar.drain(), "eRho", 1)

        @block.tensor
        def _(tensor):
            tensor.wait_ge(sems["eRho"], 19)
            tensor.wait_ge(sems["eEvacXn"], 127)
            inc(tensor.transpose(pTA[0:16, 0:128], scrR[:, 64:80], IDN), "eTrans", 1)
        TR_RHO = cnt["eTrans"]

        @block.scalar
        def _(scalar):
            scalar.wait_ge(sems["eTrans"], TR_RHO)
            scalar.copy(t16, pTA[0:16, 0:128])
            inc(scalar.drain(), "eEvacXn", 1)
        EV_RHO = cnt["eEvacXn"]

        @block.scalar
        def _(scalar):
            scalar.wait_ge(sems["eEvacXn"], EV_RHO)
            for ch in range(16):
                amt = inc(scalar.dma_start(out=rolrow[0:1, 128 * ch:128 * (ch + 1)],
                                           in_=t16[ch:ch + 1, 0:128]), "sRhoL", 16)

        @block.tensor
        def _(tensor):
            tensor.wait_ge(sems["sRhoL"], 256)
            for q in range(4):
                pf = pF0 if q % 2 == 0 else pF1
                if q >= 2:
                    tensor.wait_ge(sems["eLmul"], q)
                inc(tensor.matmul(pf[:, 0:512], ONE_ROW,
                                  rolrow[0:1, 512 * q:512 * (q + 1)],
                                  start=True, stop=True), "eYb", 1)

        @block.vector
        def _(vector):
            for q in range(4):
                pf = pF0 if q % 2 == 0 else pF1
                vector.wait_ge(sems["eYb"], q + 1)
                vector.tensor_copy(rowb[:, 512 * q:512 * (q + 1)], pf[:, 0:512])
                inc(vector.drain(), "eLmul", 1)

        @block.vector
        def _(vector):
            vector.wait_ge(sems["sLdA"], 16 * 20)
            for k in range(8):
                vector.tensor_scalar_mul(tcol[:, 2:3], vec_sb[:, 84 + k:85 + k], SQD)
                vector.drain()
                vector.tensor_scalar_mul(big4[:, 512 * k:512 * (k + 1)],
                                         big4[:, 512 * k:512 * (k + 1)],
                                         tcol[:, 2:3])
                inc(vector.drain(), "eLmul", 1)

        TR_D0 = cnt["eTrans"]
        for tt in range(4):
            for ech in range(8):
                g = tt * 8 + ech
                pd = pD0 if g % 2 == 0 else pD1

                @block.tensor
                def _(tensor, tt=tt, ech=ech, g=g, pd=pd):
                    tensor.wait_ge(sems["eEvacXn"], 8 * (4 * tt + 4))
                    if g >= 2:
                        tensor.wait_ge(sems["eDelta"], g - 1)
                    for k in range(8):
                        inc(tensor.matmul(
                            pd[:, 0:512],
                            big1[:, 1024 * k + 128 * ech:1024 * k + 128 * (ech + 1)],
                            big0[:, 2048 * k + 512 * tt:2048 * k + 512 * (tt + 1)],
                            start=(k == 0), stop=(k == 7)), "eTrans", 1)

                @block.scalar
                def _(scalar, g=g, ech=ech, pd=pd):
                    scalar.wait_ge(sems["eTrans"], TR_D0 + 8 * (g + 1))
                    if g >= 2:
                        scalar.wait_ge(sems["eU"], g - 1)
                        scalar.wait_ge(sems["sDW0" if g % 2 == 0 else "sDW1"],
                                       32 * ((g - 2) // 2 + 1))
                    dl = dlu[:, 1024 * (g % 2):1024 * (g % 2) + 512]
                    scalar.activation(scrB[:, 0:512], pd[:, 0:512], AF.Exp,
                                      bias=vec_sb[:, ech:ech + 1])
                    scalar.drain()
                    scalar.activation(dl, scrB[:, 0:512], AF.Ln, bias=1.0)
                    inc(scalar.drain(), "eDelta", 1)

                @block.vector
                def _(vector, tt=tt, ech=ech, g=g):
                    dl = dlu[:, 1024 * (g % 2):1024 * (g % 2) + 512]
                    ut = dlu[:, 1024 * (g % 2) + 512:1024 * (g % 2) + 1024]
                    vector.wait_ge(sems["eDelta"], g + 1)
                    vector.tensor_mul(
                        ut, dl,
                        big0[:, 2048 * ech + 512 * tt:2048 * ech + 512 * (tt + 1)])
                    inc(vector.drain(), "eU", 1)

        @block.sync
        def _(sync):
            for tt in range(4):
                for ech in range(8):
                    g = tt * 8 + ech
                    dl = dlu[:, 1024 * (g % 2):1024 * (g % 2) + 512]
                    ut = dlu[:, 1024 * (g % 2) + 512:1024 * (g % 2) + 1024]
                    sync.wait_ge(sems["eU"], g + 1)
                    par = "sDW0" if g % 2 == 0 else "sDW1"
                    inc(sync.dma_start(
                        out=agd_in.ap()[128 * ech:128 * (ech + 1),
                                        512 * tt:512 * (tt + 1)], in_=dl), par, 16)
                    inc(sync.dma_start(
                        out=agu_in.ap()[128 * ech:128 * (ech + 1),
                                        512 * tt:512 * (tt + 1)], in_=ut), par, 16)

        TR_S0 = cnt["eTrans"]
        for tt in range(4):

            @block.tensor
            def _(tensor, tt=tt):
                if tt == 0:
                    tensor.wait_ge(sems["eEvacXn"], 128)
                if tt >= 1:
                    tensor.wait_ge(sems["eSrow"], tt)
                for k in range(8):
                    inc(tensor.matmul(pS, vec_sb[:, 76 + k:77 + k],
                                      big0[:, 2048 * k + 512 * tt:2048 * k + 512 * (tt + 1)],
                                      start=(k == 0), stop=(k == 7)), "eTrans", 1)

            @block.vector
            def _(vector, tt=tt):
                vector.wait_ge(sems["eTrans"], TR_S0 + 8 * (tt + 1))
                if tt >= 1:
                    vector.wait_ge(sems["sSw"], 16 * tt)
                vector.tensor_copy(sev, pS)
                inc(vector.drain(), "eSrow", 1)

            @block.sync
            def _(sync, tt=tt):
                sync.wait_ge(sems["eSrow"], tt + 1)
                inc(sync.dma_start(out=ag_s_in.ap()[0:1, 512 * tt:512 * (tt + 1)],
                                   in_=sev), "sSw", 16)

        TR_L0 = cnt["eTrans"]
        for hch in range(4):
            for tt in range(4):
                g = hch * 4 + tt
                pl = pL0 if g % 2 == 0 else pL1

                @block.tensor
                def _(tensor, hch=hch, tt=tt, g=g, pl=pl):
                    tensor.wait_ge(sems["eLmul"], 13)
                    if g >= 2:
                        tensor.wait_ge(sems["eLmul"], 13 + g - 1)
                    for k in range(8):
                        inc(tensor.matmul(
                            pl[:, 0:512],
                            big4[:, 512 * k + 128 * hch:512 * k + 128 * (hch + 1)],
                            big0[:, 2048 * k + 512 * tt:2048 * k + 512 * (tt + 1)],
                            start=(k == 0), stop=(k == 7)), "eTrans", 1)

                @block.vector
                def _(vector, g=g, tt=tt, pl=pl):
                    vector.wait_ge(sems["eTrans"], TR_L0 + 8 * (g + 1))
                    if g >= 2:
                        vector.wait_ge(sems["eLH1"], g - 1)
                    vector.tensor_mul(scrC, pl[:, 0:512],
                                      rowb[:, 512 * tt:512 * (tt + 1)])
                    inc(vector.drain(), "eLmul", 1)

                @block.scalar
                def _(scalar, hch=hch, tt=tt, g=g):
                    scalar.wait_ge(sems["eLmul"], 13 + g + 1)
                    scalar.activation(
                        big2[:, 2048 * hch + 512 * tt:2048 * hch + 512 * (tt + 1)],
                        scrC, AF.Gelu,
                        bias=vec_sb[:, 16 + hch:17 + hch])
                    inc(scalar.drain(), "eLH1", 1)

        @block.gpsimd
        def _(gpsimd):
            gpsimd.wait_ge(sems["sSw"], 64)
            inc(gpsimd.collective_compute(
                "AllGather", ALU.bypass, replica_groups=[list(range(NCORE))],
                ins=[ag_s_in.ap().opt()], outs=[ag_s_out.ap().opt()]), "ccS", 1)
            gpsimd.wait_ge(sems["sDW0"], 16 * 32)
            gpsimd.wait_ge(sems["sDW1"], 16 * 32)
            for h in range(2):
                inc(gpsimd.collective_compute(
                    "AllGather", ALU.bypass, replica_groups=[list(range(NCORE))],
                    ins=[agd_in.ap()[512 * h:512 * (h + 1), :].opt()],
                    outs=[agd_out.ap()[4096 * h:4096 * (h + 1), :].opt()]), "ccDU", 1)
                inc(gpsimd.collective_compute(
                    "AllGather", ALU.bypass, replica_groups=[list(range(NCORE))],
                    ins=[agu_in.ap()[512 * h:512 * (h + 1), :].opt()],
                    outs=[agu_out.ap()[4096 * h:4096 * (h + 1), :].opt()]), "ccDU", 1)

        @block.sync
        def _(sync):
            sync.wait_ge(sems["ccS"], 1)
            for b in range(Bsz):
                inc(sync.dma_start(
                    out=s128[:, 64 * b:64 * (b + 1)],
                    in_=ag_s_out.ap()[4 * b:4 * b + 4, :]
                        .rearrange("r t -> (r t)").rearrange("(p f) -> p f", p=128)),
                    "sSw", 16)

        @block.vector
        def _(vector):
            vector.wait_ge(sems["sSw"], 96)
            for b in range(Bsz):
                vector.reduce_max(t2a[:, b:b + 1], s128[:, 64 * b:64 * (b + 1)], AX.X)
            vector.drain()
            vector.tensor_copy(t2b[:, 0:2], t2a[:, 0:2])
            inc(vector.drain(), "rtd", 1)

        @block.tensor
        def _(tensor):
            tensor.wait_ge(sems["rtd"], 1)
            tensor.wait_ge(sems["eEvacXn"], EV_RHO)
            inc(tensor.transpose(pTA[0:2, 0:128], t2b[:, 0:2], IDN), "rtp", 1)

        @block.vector
        def _(vector):
            vector.wait_ge(sems["rtp"], 1)
            vector.tensor_copy(tr2[:, 0:128], pTA[0:2, 0:128])
            vector.drain()
            vector.reduce_max(t2a[0:2, 0:1], tr2[:, 0:128], AX.X)
            inc(vector.drain(), "rtd", 1)

        @block.tensor
        def _(tensor):
            tensor.wait_ge(sems["rtd"], 2)
            inc(tensor.transpose(pTB[0:1, 0:2], t2a[0:2, 0:1], IDN[0:2, 0:2]), "rtp", 1)

        @block.vector
        def _(vector):
            vector.wait_ge(sems["rtp"], 2)
            vector.tensor_copy(ms1[0:1, 0:2], pTB[0:1, 0:2])
            inc(vector.drain(), "rtd", 1)

        @block.tensor
        def _(tensor):
            tensor.wait_ge(sems["rtd"], 3)
            inc(tensor.matmul(pR, ONE_ROW, ms1[0:1, 0:2], start=True, stop=True),
                "rtp", 1)

        @block.vector
        def _(vector):
            vector.wait_ge(sems["rtp"], 3)
            vector.tensor_copy(m2[:, 0:2], pR)
            vector.memset(na2[:], -A1)
            vector.drain()
            vector.tensor_tensor(mm2[:], m2[:], na2[:], ALU.min)
            inc(vector.drain(), "rtd", 1)

        RTD0, RTA0, RTP0 = cnt["rtd"], cnt["rta"], cnt["rtp"]
        for j in range(NIT - 1):

            @block.vector
            def _(vector, j=j):
                if j == 0:
                    vector.wait_ge(sems["sRhoL"], 256)
                if j > 0:
                    vector.wait_ge(sems["rta"], RTA0 + 2 * j)
                    vector.scalar_tensor_tensor(na2[:], t2a[:, 0:2], NLOGK,
                                                mm2[:], ALU.add, ALU.add)
                    vector.drain()
                    vector.tensor_tensor(mm2[:], m2[:], na2[:], ALU.min)
                    vector.drain()
                vector.tensor_tensor(v3, s3, bc2(na2[:]), ALU.min)
                vector.drain()
                vector.tensor_tensor(w3, v3, bc2(mm2[:]), ALU.subtract)
                inc(vector.drain(), "rtd", 1)

            @block.scalar
            def _(scalar, j=j):
                scalar.wait_ge(sems["rtd"], RTD0 + 2 * j + 1)
                scalar.activation(e128[:], w128[:], AF.Exp)
                inc(scalar.drain(), "rta", 1)

            @block.vector
            def _(vector, j=j):
                vector.wait_ge(sems["rta"], RTA0 + 2 * j + 1)
                vector.reduce_sum(rs2[:, 0:2], e3, AX.X)
                inc(vector.drain(), "rtd", 1)

            @block.tensor
            def _(tensor, j=j):
                tensor.wait_ge(sems["rtd"], RTD0 + 2 * j + 2)
                if j > 0:
                    tensor.wait_ge(sems["rta"], RTA0 + 2 * j)
                inc(tensor.matmul(pR, ONES, rs2[:, 0:2], start=True, stop=True),
                    "rtp", 1)

            @block.scalar
            def _(scalar, j=j):
                scalar.wait_ge(sems["rtp"], RTP0 + j + 1)
                scalar.activation(t2a[:, 0:2], pR, AF.Ln)
                inc(scalar.drain(), "rta", 1)

        @block.vector
        def _(vector):
            vector.wait_ge(sems["rta"], RTA0 + 2 * (NIT - 1))
            vector.scalar_tensor_tensor(na2[:], t2a[:, 0:2], NLOGK, mm2[:],
                                        ALU.add, ALU.add)
            vector.drain()
            vector.tensor_tensor(v3, s3, bc2(na2[:]), ALU.subtract)
            vector.drain()
            vector.tensor_scalar_min(v128[:], v128[:], 0.0)
            vector.memset(w128[:], 0.0)
            vector.drain()
            vector.tensor_sub(w128[:], w128[:], v128[:])
            vector.memset(i2a[:], 0)
            vector.memset(i2b[:], 2139095039)
            inc(vector.drain(), "btd", 1)

        def bisect_hi_update(vector):
            vector.tensor_single_scalar(sc2[:], t2b[:, 0:2], float(KH), ALU.is_ge)
            vector.drain()
            vector.tensor_copy(i2e[:], sc2[:])
            vector.tensor_sub(i2d[:], i2c[:], i2b[:])
            vector.drain()
            vector.tensor_mul(i2d[:], i2d[:], i2e[:])
            vector.drain()
            vector.tensor_add(i2b[:], i2b[:], i2d[:])
            vector.drain()

        for j in range(31):

            @block.vector
            def _(vector, j=j):
                if j > 0:
                    vector.wait_ge(sems["bta"], j)
                    bisect_hi_update(vector)
                    vector.tensor_scalar(i2e[:], i2e[:], -1, 1, ALU.mult, ALU.add)
                    vector.tensor_sub(i2d[:], i2c[:], i2a[:])
                    vector.drain()
                    vector.tensor_scalar_add(i2d[:], i2d[:], 1)
                    vector.drain()
                    vector.tensor_mul(i2d[:], i2d[:], i2e[:])
                    vector.drain()
                    vector.tensor_add(i2a[:], i2a[:], i2d[:])
                    vector.drain()
                vector.tensor_sub(i2c[:], i2b[:], i2a[:])
                vector.drain()
                vector.tensor_single_scalar(i2c[:], i2c[:], 1, ALU.logical_shift_right)
                vector.drain()
                vector.tensor_add(i2c[:], i2c[:], i2a[:])
                vector.drain()
                vector.tensor_tensor(e3, key3, bc2(i2c[:]), ALU.is_le)
                vector.drain()
                vector.reduce_sum(rs2[:, 0:2], e3, AX.X)
                inc(vector.drain(), "btd", 1)

            @block.tensor
            def _(tensor, j=j):
                tensor.wait_ge(sems["btd"], j + 2)
                if j > 0:
                    tensor.wait_ge(sems["bta"], j)
                inc(tensor.matmul(pR, ONES, rs2[:, 0:2], start=True, stop=True),
                    "btp", 1)

            @block.scalar
            def _(scalar, j=j):
                scalar.wait_ge(sems["btp"], j + 1)
                scalar.copy(t2b[:, 0:2], pR)
                inc(scalar.drain(), "bta", 1)

        @block.vector
        def _(vector):
            vector.wait_ge(sems["bta"], 31)
            bisect_hi_update(vector)
            vector.tensor_copy(thk[:], i2b[:])
            vector.drain()
            vector.tensor_tensor(e3, key3, bc2(thk[:]), ALU.is_lt)
            vector.drain()
            vector.reduce_sum(rs2[:, 0:2], e3, AX.X)
            inc(vector.drain(), "btd", 1)

        @block.tensor
        def _(tensor):
            tensor.wait_ge(sems["btd"], 33)
            tensor.wait_ge(sems["bta"], 31)
            inc(tensor.matmul(pR, ONES, rs2[:, 0:2], start=True, stop=True), "btp", 1)

        @block.scalar
        def _(scalar):
            scalar.wait_ge(sems["btp"], 32)
            scalar.copy(t2b[:, 0:2], pR)
            inc(scalar.drain(), "bta", 1)

        @block.vector
        def _(vector):
            vector.wait_ge(sems["bta"], 32)
            vector.tensor_scalar(tt2[:], t2b[:, 0:2], -1.0, float(KH),
                                 ALU.mult, ALU.add)
            vector.tensor_tensor(e3, key3, bc2(thk[:]), ALU.is_lt)
            vector.tensor_tensor(v3, key3, bc2(thk[:]), ALU.is_equal)
            vector.drain()
            for b in range(Bsz):
                vector.tensor_tensor_scan(w128[:, 64 * b:64 * (b + 1)],
                                          ONES[:, 0:64], v128[:, 64 * b:64 * (b + 1)],
                                          0.0, ALU.mult, ALU.add)
            vector.drain()
            vector.tensor_copy(rs2[:, 0:1], w128[:, 63:64])
            vector.tensor_copy(rs2[:, 1:2], w128[:, 127:128])
            inc(vector.drain(), "eSel", 1)

        @block.tensor
        def _(tensor):
            tensor.wait_ge(sems["eSel"], 1)
            inc(tensor.matmul(pR, LSTR, rs2[:, 0:2], start=True, stop=True),
                "eSel", 1)

        @block.vector
        def _(vector):
            vector.wait_ge(sems["eSel"], 2)
            vector.tensor_copy(t2a[:, 0:2], pR)
            vector.drain()
            vector.tensor_sub(w128[:], w128[:], v128[:])
            vector.drain()
            vector.tensor_tensor(w3, w3, bc2(t2a[:, 0:2]), ALU.add)
            vector.drain()
            vector.tensor_tensor(w3, w3, bc2(tt2[:]), ALU.is_lt)
            vector.drain()
            vector.tensor_mul(v128[:], v128[:], w128[:])
            vector.drain()
            vector.tensor_add(e128[:], e128[:], v128[:])
            vector.drain()
            for b in range(Bsz):
                vector.tensor_tensor_scan(w128[:, 64 * b:64 * (b + 1)],
                                          ONES[:, 0:64], e128[:, 64 * b:64 * (b + 1)],
                                          0.0, ALU.mult, ALU.add)
            vector.drain()
            vector.tensor_copy(rs2[:, 0:1], w128[:, 63:64])
            vector.tensor_copy(rs2[:, 1:2], w128[:, 127:128])
            inc(vector.drain(), "eSel", 1)

        @block.tensor
        def _(tensor):
            tensor.wait_ge(sems["eSel"], 3)
            inc(tensor.matmul(pR, LSTR, rs2[:, 0:2], start=True, stop=True),
                "eSel", 1)

        @block.vector
        def _(vector):
            vector.wait_ge(sems["eSel"], 4)
            vector.tensor_copy(t2a[:, 0:2], pR)
            vector.drain()
            vector.tensor_sub(w128[:], w128[:], e128[:])
            vector.drain()
            vector.tensor_tensor(w3, w3, bc2(t2a[:, 0:2]), ALU.add)
            vector.drain()
            vector.tensor_scalar_add(w128[:, 64:128], w128[:, 64:128], float(KH))
            vector.drain()
            vector.tensor_scalar_add(w128[:], w128[:], -BIGRANK)
            vector.drain()
            vector.tensor_mul(w128[:], w128[:], e128[:])
            vector.drain()
            vector.tensor_scalar_add(w128[:], w128[:], BIGRANK)
            inc(vector.drain(), "eSel", 1)

        @block.sync
        def _(sync):
            sync.wait_ge(sems["eSel"], 5)
            for b in range(Bsz):
                inc(sync.dma_start(
                    out=rank_dram.ap()[b, :].rearrange("(p f) -> p f", p=128),
                    in_=w128[:, 64 * b:64 * (b + 1)]), "sMisc", 16)
        MISC_RANK = cnt["sMisc"]

        @block.scalar
        def _(scalar):
            scalar.wait_ge(sems["sMisc"], MISC_RANK)
            g0 = scalar.value_load(crow_sb[0:1, 2:3])
            rd_flat = rank_dram.ap().rearrange("b t -> (b t)")
            inc(scalar.dma_start(out=rk_loc.ap(),
                                 in_=bass.AP(tensor=rd_flat.tensor, offset=g0,
                                             ap=[[2048, 1], [1, 2048]])), "sRflR", 16)
            scalar.wait_ge(sems["sRflR"], 16)
            inc(scalar.dma_start(out=rkrow[0:1, :], in_=rk_loc.ap()), "sRkL", 16)
            for i in range(16):
                inc(scalar.dma_start(
                    out=rkc[:, i:i + 1],
                    in_=rk_loc.ap()[0, 128 * i:128 * (i + 1)]
                        .rearrange("(p o) -> p o", o=1)), "sRkL", 16)

        @block.vector
        def _(vector):
            vector.wait_ge(sems["sRkL"], 16 * 17)
            vector.tensor_reduce(pm1[0:1, 0:1], rkrow[0:1, :], AX.X, ALU.min)
            vector.drain()
            vector.tensor_scalar_min(pm1[0:1, 0:1], pm1[0:1, 0:1], float(NHG - 1))
            vector.drain()
            vector.tensor_copy(pm1i[0:1, 0:1], pm1[0:1, 0:1])
            vector.tensor_single_scalar(rkrow[0:1, :], rkrow[0:1, :],
                                        pm1[0:1, 0:1], ALU.subtract)
            vector.tensor_copy(rkci[:], rkc[:])
            inc(vector.drain(), "eSel", 1)

        @block.sync
        def _(sync):
            for i in range(16):
                inc(sync.dma_start(out=xsc.ap()[128 * i:128 * (i + 1), :],
                                   in_=zrow), "sZero", 16)
            for i in range(4):
                inc(sync.dma_start(out=h2full.ap()[NHG + 128 * i:NHG + 128 * (i + 1), :],
                                   in_=zrow[:, 0:Dm]), "sPadW", 16)
            for i in range(16):
                if i == 0:
                    sync.wait_ge(sems["eTrans"], 128)
                if i >= 1:
                    sync.wait_ge(sems["sXscW"], 32 * i)
                inc(sync.dma_start(out=xch[:],
                                   in_=x_tm[128 * i:128 * (i + 1), :]), "sXrl", 16)

        @block.gpsimd
        def _(gpsimd):
            gpsimd.wait_ge(sems["eSel"], 6)
            gpsimd.wait_ge(sems["sZero"], 256)
            for i in range(16):
                gpsimd.wait_ge(sems["sXrl"], 16 * (i + 1))
                inc(gpsimd.indirect_dma_start(
                    xsc.ap(), IndirectOffsetOnAxis(rkci[:, i:i + 1], 0),
                    xch[:], None,
                    bounds_check=NHG - 1, oob_is_err=False), "sXscW", 16)
                inc(gpsimd.indirect_dma_start(
                    xsc.ap(), IndirectOffsetOnAxis(rkci[:, i:i + 1], 0),
                    mv[:, 2 * i:2 * i + 2], None, element_offset=1024,
                    bounds_check=NHG - 1, oob_is_err=False), "sXscW", 16)
            gpsimd.wait_ge(sems["sXscW"], 16 * 32)
            for q in range(4):
                inc(gpsimd.collective_compute(
                    "AllReduce", ALU.add, replica_groups=[list(range(NCORE))],
                    ins=[xsc.ap()[512 * q:512 * (q + 1), :].opt()],
                    outs=[xsc_sum.ap()[512 * q:512 * (q + 1), :].opt()]), "ccAR", 1)

        @block.scalar
        def _(scalar):
            e0k = scalar.value_load(crow_sb[0:1, 1:2])
            scalar.wait_ge(sems["ccDU"], 4)
            inc(scalar.dma_start(
                out=d_loc.ap().rearrange("(r p) f -> r p f", r=8),
                in_=bass.AP(tensor=agd_out.ap().tensor, offset=e0k,
                            ap=[[1024 * 1024, 8], [2048, 128], [1, 2048]])),
                "sRfl0", 16)
            inc(scalar.dma_start(
                out=u_loc.ap().rearrange("(r p) f -> r p f", r=8),
                in_=bass.AP(tensor=agu_out.ap().tensor, offset=e0k,
                            ap=[[1024 * 1024, 8], [2048, 128], [1, 2048]])),
                "sRfl0", 16)

            def emit_load(sup):
                r, h = sup // 2, sup % 2
                if sup >= 2:
                    scalar.wait_ge(sems["eSup"], sup - 1)
                scalar.wait_ge(sems["sRfl0"], 32)
                par = sup % 2
                sl_ = "sScanL0" if par == 0 else "sScanL1"
                inc(scalar.dma_start(
                    out=DLT[par],
                    in_=d_loc.ap()[128 * r:128 * (r + 1),
                                   1024 * h:1024 * (h + 1)]), sl_, 16)
                inc(scalar.dma_start(
                    out=USB[par],
                    in_=u_loc.ap()[128 * r:128 * (r + 1),
                                   1024 * h:1024 * (h + 1)]), sl_, 16)

            emit_load(0)
            emit_load(1)
            for sup in range(16):
                for s in range(Sst):
                    idx = sup * 16 + s
                    scalar.wait_ge(sems["sScanL0" if sup % 2 == 0 else "sScanL1"],
                                   32 * (sup // 2 + 1))
                    if idx >= 2:
                        scalar.wait_ge(sems["eScan"], idx - 1)
                    inc(scalar.activation(ABR[idx % 2], DLT[sup % 2], AF.Exp,
                                          scale=napos[:, s:s + 1]), "eExp", 1)
                    if s == 0 and sup >= 1:
                        scalar.wait_ge(sems["eSup"], sup)
                        su = sup - 1
                        inc(scalar.dma_start(
                            out=agy_in.ap()[128 * (su // 4):128 * (su // 4) + 128,
                                            1024 * (su % 4):1024 * (su % 4) + 1024],
                            in_=YSB[1]), "sYW%d" % (su // 4), 16)
                if sup + 2 < 16:
                    emit_load(sup + 2)
            scalar.wait_ge(sems["eSup"], 16)
            inc(scalar.dma_start(out=agy_in.ap()[384:512, 3072:4096],
                                 in_=YSB[1]), "sYW3", 16)

        for sup in range(16):

            @block.vector
            def _(vector, sup=sup):
                par = sup % 2
                for s in range(Sst):
                    idx = sup * 16 + s
                    vector.wait_ge(sems["eExp"], idx + 1)
                    if s == 0 and sup % 8 == 0:
                        vector.memset(state[:], 0.0)
                        vector.drain()
                    if s == 1 and sup >= 1:
                        vector.wait_ge(sems["sYW%d" % ((sup - 1) // 4)],
                                       16 * ((sup - 1) % 4 + 1))
                    inc(vector.tensor_tensor_scan(
                        GSB[idx % 2], ABR[idx % 2], USB[par],
                        state[:, s:s + 1], ALU.mult, ALU.add), "eScan", 1)
                    if s == 0:
                        vector.tensor_scalar_mul(YSB[0], GSB[idx % 2],
                                                 apb_sb[:, 16 + s:17 + s])
                    else:
                        vector.scalar_tensor_tensor(
                            YSB[s % 2], GSB[idx % 2], apb_sb[:, 16 + s:17 + s],
                            YSB[(s + 1) % 2], ALU.mult, ALU.add)
                    vector.tensor_copy(state[:, s:s + 1], GSB[idx % 2][:, 1023:1024])
                vector.tensor_copy(tcol[:, 3:4], state[:, 0:1])
                inc(vector.drain(), "eSup", 1)

        @block.gpsimd
        def _(gpsimd):
            for q in range(4):
                gpsimd.wait_ge(sems["sYW%d" % q], 16 * 4)
                inc(gpsimd.collective_compute(
                    "AllGather", ALU.bypass, replica_groups=[list(range(NCORE))],
                    ins=[agy_in.ap()[128 * q:128 * (q + 1), :].opt()],
                    outs=[agy_all.ap()[1024 * q:1024 * (q + 1), :].opt()]), "ccY", 1)

        for k in range(2):

            @block.scalar
            def _(scalar, k=k):
                if k == 0:
                    scalar.wait_ge(sems["ccAR"], 4)
                else:
                    scalar.wait_ge(sems["eXgT"], 8 * k)
                s0x = scalar.value_load(crow_sb[0:1, 3:4])
                inc(scalar.dma_start(
                    out=XG_H,
                    in_=bass.AP(tensor=xsc_sum.ap().tensor,
                                offset=s0x + 128 * XCP * k,
                                ap=[[XCP, 128], [1, XCP]])), "sXgL", 16)

            @block.scalar
            def _(scalar, k=k):
                scalar.wait_ge(sems["sXgL"], 16 * (k + 1))
                vcol = XG_H[:, 1025:1026]
                scalar.activation(tcol[:, 0:1], vcol, AF.Ln, bias=EPS_LN)
                scalar.drain()
                scalar.activation(rhoc[:, k:k + 1], tcol[:, 0:1], AF.Exp, scale=-0.5)
                scalar.activation(rhoc[:, 4 + k:5 + k], tcol[:, 0:1], AF.Exp,
                                  scale=-1.0)
                inc(scalar.drain(), "eXgP", 1)

            @block.vector
            def _(vector, k=k):
                vector.wait_ge(sems["eXgP"], 12 * k + 1)
                vcol = XG_H[:, 1025:1026]
                vector.tensor_mul(tcol[:, 0:1], vcol, rhoc[:, 4 + k:5 + k])
                vector.drain()
                vector.tensor_scalar_mul(tcol[:, 0:1], tcol[:, 0:1], float(Dm))
                vector.drain()
                vector.tensor_scalar_max(rhoc[:, 8 + k:9 + k], tcol[:, 0:1],
                                         1e-24)
                inc(vector.drain(), "eXgP", 1)

            @block.scalar
            def _(scalar, k=k):
                scalar.wait_ge(sems["eXgP"], 12 * k + 2)
                scalar.activation(tcol[:, 0:1], rhoc[:, 8 + k:9 + k], AF.Ln)
                scalar.drain()
                scalar.activation(rhoc[:, 12 + k:13 + k], tcol[:, 0:1], AF.Exp,
                                  scale=-0.5)
                inc(scalar.drain(), "eXgP", 1)

            @block.vector
            def _(vector, k=k):
                vector.wait_ge(sems["eXgP"], 12 * k + 3)
                xk = XG_H[:, 0:1024]
                vector.tensor_mul(tcol[:, 0:1], rhoc[:, k:k + 1],
                                  rhoc[:, 12 + k:13 + k])
                vector.drain()
                vector.tensor_scalar_mul(tcol[:, 0:1], tcol[:, 0:1], SQD)
                vector.drain()
                vector.tensor_scalar(xk, xk, XG_H[:, 1024:1025],
                                     tcol[:, 0:1], ALU.subtract, ALU.mult)
                inc(vector.drain(), "eXgP", 1)

            @block.tensor
            def _(tensor, k=k):
                tensor.wait_ge(sems["eXgP"], 12 * k + 4)
                for j in range(8):
                    idx = k * 8 + j
                    if idx >= 2:
                        tensor.wait_ge(sems["eXgT"], idx - 1)
                    inc(tensor.transpose(
                        pT[idx % 2], XG_H[:, 128 * j:128 * (j + 1)], IDN),
                        "eXgP", 1)

            @block.scalar
            def _(scalar, k=k):
                for j in range(8):
                    idx = k * 8 + j
                    scalar.wait_ge(sems["eXgP"], 12 * k + 4 + j + 1)
                    scalar.activation(
                        XGN[:, 256 * j + 128 * k:256 * j + 128 * (k + 1)],
                        pT[idx % 2], AF.Copy, scale=vec_sb[:, 68 + j:69 + j])
                    inc(scalar.drain(), "eXgT", 1)

        @block.sync
        def _(sync):
            sync.wait_ge(sems["ccAR"], 4)
            for v in range(64):
                m, half = v // 2, v % 2
                if v >= 2:
                    sync.wait_ge(sems["eH1m"], 4 * (v - 1))
                r0 = 1024 * m + 512 * half
                inc(sync.dma_start(
                    out=big5[:, 512 * half:512 * (half + 1)]
                        .rearrange("p (k d) -> p k d", k=4),
                    in_=w1hb[r0:r0 + 512, :]
                        .rearrange("(k p) d -> p k d", p=128)), "sW1L", 16)

        for m in range(32):
            pl = pD0 if m % 2 == 0 else pD1

            @block.tensor
            def _(tensor, m=m, pl=pl):
                tensor.wait_ge(sems["eXgT"], 16)
                if m >= 2:
                    tensor.wait_ge(sems["eGelH"], m - 1)
                for k in range(8):
                    if k == 0:
                        tensor.wait_ge(sems["sW1L"], 16 * (2 * m + 1))
                    if k == 4:
                        tensor.wait_ge(sems["sW1L"], 16 * (2 * m + 2))
                    inc(tensor.matmul(
                        pl[:, 0:256],
                        big5[:, 128 * k:128 * (k + 1)],
                        XGN[:, 256 * k:256 * (k + 1)],
                        start=(k == 0), stop=(k == 7)), "eH1m", 1)

            @block.scalar
            def _(scalar, m=m, pl=pl):
                scalar.wait_ge(sems["eH1m"], 8 * (m + 1))
                scalar.activation(big1[:, 256 * m:256 * (m + 1)], pl[:, 0:256],
                                  AF.Gelu, bias=vec_sb[:, 28 + m:29 + m])
                inc(scalar.drain(), "eGelH", 1)
        H1M_END = cnt["eH1m"]

        @block.sync
        def _(sync):
            for u in range(32):
                q, half = u // 2, u % 2
                m, hf = q // 2, q % 2
                if u >= 2:
                    sync.wait_ge(sems["eH2m"], 8 * (u - 1))
                r0 = 4096 * m + 2048 * hf + 1024 * half
                inc(sync.dma_start(
                    out=big5[:, 1024 + 1024 * half:1024 + 1024 * (half + 1)]
                        .rearrange("p (k d) -> p k d", k=8),
                    in_=w2hb[r0:r0 + 1024, :]
                        .rearrange("(k p) d -> p k d", p=128)), "sW2L", 16)

        for m in range(8):
            pl = pL0 if m % 2 == 0 else pL1

            @block.tensor
            def _(tensor, m=m, pl=pl):
                tensor.wait_ge(sems["eGelH"], 32)
                if m >= 2:
                    tensor.wait_ge(sems["eH2e"], m - 1)
                for hf in range(2):
                    for kk in range(16):
                        k = 16 * hf + kk
                        if kk == 0:
                            tensor.wait_ge(sems["sW2L"],
                                           16 * (4 * m + 2 * hf + 1))
                        if kk == 8:
                            tensor.wait_ge(sems["sW2L"],
                                           16 * (4 * m + 2 * hf + 2))
                        inc(tensor.matmul(
                            pl[:, 0:256],
                            big5[:, 1024 + 128 * kk:1024 + 128 * (kk + 1)],
                            big1[:, 256 * k:256 * (k + 1)],
                            start=(k == 0), stop=(k == 31)), "eH2m", 1)

            @block.vector
            def _(vector, m=m, pl=pl):
                vector.wait_ge(sems["eH2m"], 32 * (m + 1))
                vector.tensor_scalar_add(scrH[:, 256 * (m % 2):256 * (m % 2) + 256],
                                         pl[:, 0:256],
                                         vec_sb[:, 60 + m:61 + m])
                inc(vector.drain(), "eH2e", 1)

            @block.tensor
            def _(tensor, m=m):
                tensor.wait_ge(sems["eH2e"], m + 1)
                for kk in range(2):
                    idx = 2 * m + kk
                    if idx >= 2:
                        tensor.wait_ge(sems["eH2T"], idx - 1)
                    inc(tensor.transpose(
                        pT[idx % 2],
                        scrH[:, 256 * (m % 2) + 128 * kk:256 * (m % 2) + 128 * (kk + 1)],
                        IDN), "eH1m", 1)

            @block.vector
            def _(vector, m=m):
                for kk in range(2):
                    idx = 2 * m + kk
                    vector.wait_ge(sems["eH1m"], H1M_END + idx + 1)
                    vector.tensor_copy(
                        h2tm[:, 1024 * kk + 128 * m:1024 * kk + 128 * (m + 1)],
                        pT[idx % 2])
                    inc(vector.drain(), "eH2T", 1)

        @block.sync
        def _(sync):
            sync.wait_ge(sems["eH2T"], 16)
            inc(sync.dma_start(
                out=ag_h2_in.ap().rearrange("(kk p) d -> p kk d", p=128),
                in_=h2tm.rearrange("p (kk d) -> p kk d", kk=2)), "sH2W", 16)

        @block.gpsimd
        def _(gpsimd):
            gpsimd.wait_ge(sems["sH2W"], 16)
            gpsimd.wait_ge(sems["sPadW"], 64)
            inc(gpsimd.collective_compute(
                "AllGather", ALU.bypass, replica_groups=[list(range(NCORE))],
                ins=[ag_h2_in.ap().opt()], outs=[h2full.ap()[0:NHG, :].opt()]),
                "ccH2", 1)

        @block.scalar
        def _(scalar):
            scalar.wait_ge(sems["ccH2"], 1)
            pv = scalar.value_load(pm1i[0:1, 0:1])
            inc(scalar.dma_start(
                out=H2OWN[:].rearrange("p (k f) -> p k f", k=4),
                in_=bass.AP(tensor=h2full.ap().tensor, offset=pv * Dm,
                            ap=[[Dm, 128], [128 * Dm, 4], [1, Dm]])), "sOwnL", 16)

        @block.tensor
        def _(tensor):
            tensor.wait_ge(sems["eSel"], 6)
            tensor.wait_ge(sems["eLmul"], 29)
            for q in range(4):
                pf = pF0 if q % 2 == 0 else pF1
                if q >= 2:
                    tensor.wait_ge(sems["eRelB"], q - 1)
                inc(tensor.matmul(pf[:, 0:512], ONE_ROW,
                                  rkrow[0:1, 512 * q:512 * (q + 1)],
                                  start=True, stop=True), "eYb", 1)

        @block.vector
        def _(vector):
            for q in range(4):
                pf = pF0 if q % 2 == 0 else pF1
                vector.wait_ge(sems["eYb"], 4 + q + 1)
                vector.tensor_copy(rowb[:, 512 * q:512 * (q + 1)], pf[:, 0:512])
                inc(vector.drain(), "eRelB", 1)

        @block.sync
        def _(sync):
            sync.wait_ge(sems["eH2m"], 8 * 32)
            for k in range(8):
                inc(sync.dma_start(out=big1[:, 1024 * k:1024 * (k + 1)],
                                   in_=woT[128 * k:128 * (k + 1), :]), "sFinW", 16)
            sync.wait_ge(sems["eLH1"], 16)
            for k in range(4):
                inc(sync.dma_start(out=big4[:, 1024 * k:1024 * (k + 1)],
                                   in_=w2lT[128 * k:128 * (k + 1), :]), "sFinW", 16)



        @block.sync
        def _(sync):
            sync.wait_ge(sems["eH1m"], 256)
            yoff = [sync.value_load(crow_sb[0:1, 4 + t:5 + t]) for t in range(4)]
            sync.wait_ge(sems["ccY"], 4)
            sync.wait_ge(sems["sH2W"], 16)
            for tt in range(4):
                if tt >= 1:
                    sync.wait_ge(sems["eFinM"], 16 * 8 * tt)
                inc(sync.dma_start(
                    out=ybig[:].rearrange("p (k f) -> p k f", k=8),
                    in_=bass.AP(tensor=agy_all.ap().tensor, offset=yoff[tt],
                                ap=[[4096, 128], [128 * 4096, 8], [1, 512]])),
                    "sFinY", 16)
                for dch in range(8):
                    g = tt * 8 + dch
                    inc(sync.dma_start(
                        out=xfin[:, 512 * (g % 2):512 * (g % 2 + 1)],
                        in_=x_fm[128 * dch:128 * (dch + 1),
                                 512 * tt:512 * (tt + 1)]), "sFinX", 16)
                    sync.wait_ge(sems["eFin"], g + 1)
                    inc(sync.dma_start(
                        out=out_ext[128 * dch:128 * (dch + 1),
                                    512 * tt:512 * (tt + 1)],
                        in_=outt[:, 512 * (g % 2):512 * (g % 2 + 1)]), "sOut", 16)

        for tt in range(4):

            @block.vector
            def _(vector, tt=tt):
                vector.wait_ge(sems["eRelB"], 4)
                if tt >= 1:
                    vector.wait_ge(sems["eFinM"], 16 * 8 * tt)
                for sc in range(4):
                    vector.tensor_scalar(Stt[:, 512 * sc:512 * (sc + 1)],
                                         rowb[:, 512 * tt:512 * (tt + 1)],
                                         IOTC[:, sc:sc + 1], None, ALU.is_equal)
                    inc(vector.drain(), "eStt", 1)

            for dch in range(8):
                g = tt * 8 + dch
                pf = pF0 if g % 2 == 0 else pF1

                @block.tensor
                def _(tensor, tt=tt, dch=dch, g=g, pf=pf):
                    tensor.wait_ge(sems["sFinW"], 16 * 12)
                    tensor.wait_ge(sems["sFinY"], 16 * (tt + 1))
                    tensor.wait_ge(sems["eStt"], 4 * (tt + 1))
                    tensor.wait_ge(sems["sOwnL"], 16)
                    if g == 0:
                        tensor.wait_ge(sems["eRelB"], 4)
                    if g >= 2:
                        tensor.wait_ge(sems["eFin"], g - 1)
                    for k in range(8):
                        inc(tensor.matmul(
                            pf[:, 0:512],
                            big1[:, 1024 * k + 128 * dch:1024 * k + 128 * (dch + 1)],
                            ybig[:, 512 * k:512 * (k + 1)],
                            start=(k == 0), stop=False), "eFinM", 1)
                    for k in range(4):
                        inc(tensor.matmul(
                            pf[:, 0:512],
                            big4[:, 1024 * k + 128 * dch:1024 * k + 128 * (dch + 1)],
                            big2[:, 2048 * k + 512 * tt:2048 * k + 512 * (tt + 1)],
                            start=False, stop=False), "eFinM", 1)
                    for k in range(4):
                        inc(tensor.matmul(
                            pf[:, 0:512],
                            H2OWN[:, 1024 * k + 128 * dch:1024 * k + 128 * (dch + 1)],
                            Stt[:, 512 * k:512 * (k + 1)],
                            start=False, stop=(k == 3)), "eFinM", 1)

                @block.vector
                def _(vector, tt=tt, dch=dch, g=g, pf=pf):
                    vector.wait_ge(sems["eFinM"], 16 * (g + 1))
                    vector.wait_ge(sems["sFinX"], 16 * (g + 1))
                    vector.scalar_tensor_tensor(
                        outt[:, 512 * (g % 2):512 * (g % 2 + 1)], pf[:, 0:512],
                        bstar[:, dch:dch + 1],
                        xfin[:, 512 * (g % 2):512 * (g % 2 + 1)],
                        ALU.add, ALU.add)
                    inc(vector.drain(), "eFin", 1)

        @block.sync
        def _(sync):
            sync.wait_ge(sems["sOut"], 16 * 32)

    _es.close()
    for cm in reversed(ctx):
        cm.__exit__(None, None, None)
    return nc


_NC_CACHE = None


def _get_nc():
    global _NC_CACHE
    if _NC_CACHE is None:
        _NC_CACHE = build_nc()
    return _NC_CACHE


def kernel(L0_attn_x, L0_attn_Wd, L0_attn_bd, L0_attn_a_log, L0_attn_b_param,
           L0_attn_Wo, L0_attn_bo, L0_ffn_routing_token, L0_ffn_gamma_l,
           L0_ffn_w1_l, L0_ffn_b1_l, L0_ffn_w2_l, L0_ffn_b2_l, L0_ffn_gamma_h,
           L0_ffn_w1_h, L0_ffn_b1_h, L0_ffn_w2_h, L0_ffn_b2_h):
    f = np.asarray
    x = np.ascontiguousarray(f(L0_attn_x, dtype=np.float32)).reshape(NT, Dm)

    def colsplit(v, nch):
        v = np.ascontiguousarray(f(v, dtype=np.float32)).reshape(nch, 128)
        return v.T

    wdT = np.ascontiguousarray(f(L0_attn_Wd, dtype=np.float32).T)
    woT = np.ascontiguousarray(f(L0_attn_Wo, dtype=np.float32).T)
    w1lT = np.ascontiguousarray(f(L0_ffn_w1_l, dtype=np.float32).T)
    w2lT = np.ascontiguousarray(f(L0_ffn_w2_l, dtype=np.float32).T)
    w1hT = f(L0_ffn_w1_h, dtype=np.float32).T
    w2hT = f(L0_ffn_w2_h, dtype=np.float32).T
    w1hb = np.ascontiguousarray(
        w1hT.reshape(Dm, 32, 128).transpose(1, 0, 2)).reshape(32 * Dm, 128)
    w2hb = np.ascontiguousarray(
        w2hT.reshape(4096, 8, 128).transpose(1, 0, 2)).reshape(8 * 4096, 128)

    vecs = np.zeros((128, 92), np.float32)
    vecs[:, 0:8] = colsplit(L0_attn_bd, 8)
    vecs[:, 8:16] = colsplit(L0_attn_bo, 8)
    vecs[:, 16:20] = colsplit(L0_ffn_b1_l, 4)
    vecs[:, 20:28] = colsplit(L0_ffn_b2_l, 8)
    vecs[:, 28:60] = colsplit(L0_ffn_b1_h, 32)
    vecs[:, 60:68] = colsplit(L0_ffn_b2_h, 8)
    vecs[:, 68:76] = colsplit(L0_ffn_gamma_h, 8)
    vecs[:, 76:84] = colsplit(L0_ffn_routing_token, 8)
    vecs[:, 84:92] = colsplit(L0_ffn_gamma_l, 8)

    ones = np.ones((128, 128), np.float32)
    lstr = (np.arange(128)[:, None] < np.arange(128)[None, :]).astype(np.float32)
    idn = np.eye(128, dtype=np.float32)
    iotc = (np.arange(128)[:, None] + 128 * np.arange(4)[None, :]).astype(np.float32)
    fco = np.zeros((128, 2), np.float32)
    fco[:, 0] = EPS_LN
    fco[:, 1] = -1e-4
    consts = np.concatenate([ones, lstr, idn, iotc, fco], axis=1)

    a_log = f(L0_attn_a_log, dtype=np.float32)
    b_par = f(L0_attn_b_param, dtype=np.float32)

    in_maps = []
    for c in range(NCORE):
        g0 = TOK * c
        apbc = np.concatenate([a_log[128 * c:128 * (c + 1)],
                               b_par[128 * c:128 * (c + 1)]], axis=1)
        crow_v = np.zeros((1, 8), np.int32)
        crow_v[0, 0] = c
        crow_v[0, 1] = 8388608 * (c // 4) + 262144 * (c % 4)
        crow_v[0, 2] = g0
        crow_v[0, 3] = SLOT * c * XCP
        for t in range(4):
            gt = g0 + 512 * t
            q = gt // 4096
            col = gt % 4096
            crow_v[0, 4 + t] = (1024 * q) * 4096 + col
        in_maps.append(dict(
            x_tm=np.ascontiguousarray(x[g0:g0 + TOK]),
            x_fm=np.ascontiguousarray(x[g0:g0 + TOK].T),
            wdT=wdT, woT=woT, w1lT=w1lT, w2lT=w2lT, w1hb=w1hb, w2hb=w2hb,
            vecs=vecs, apb=np.ascontiguousarray(apbc), consts=consts, crow=crow_v,
        ))

    nc = _get_nc()
    res = run_bass_kernel_spmd(nc, in_maps, core_ids=list(range(NCORE)))
    out = np.empty((Bsz, Tseq, Dm), np.float32)
    for c in range(NCORE):
        o = res.results[c]["out"]
        b, t0 = c // 4, (c % 4) * TOK
        out[b, t0:t0 + TOK] = o.T
    return out
